# version 03
# baseline (speedup 1.0000x reference)
import numpy as np
import contextlib
import concourse.bass as bass
import concourse.mybir as mybir
from concourse.bass_utils import run_bass_kernel_spmd

F32 = mybir.dt.float32
BF16 = mybir.dt.bfloat16
I32 = mybir.dt.int32
AF = mybir.ActivationFunctionType
ALU = mybir.AluOpType
AX = mybir.AxisListType

DM = 4096
TOK = 2048
NT = TOK // 128
NS = TOK // 512
NFC = DM // 128
HEADS = 32
HD = 128
NOF = 96
SM_SCALE = 0.08838834764831845
ESHIFT = -16.0
EPS = 1e-5
C_ROUND = 12582912.0
DEBUG = False

ENGS = ("sync", "gpsimd", "tensor", "vector", "scalar")


class Sem:
    __slots__ = ("h", "n")

    def __init__(self, h):
        self.h = h
        self.n = 0


class Prog:
    def __init__(self):
        self.p = {e: [] for e in ENGS}

    def op(self, eng, fn, waits=(), inc=None, inc_amt=1):
        if inc is not None:
            inc.n += inc_amt
        v = inc.n if inc is not None else None
        waits = tuple((s, val) for (s, val) in waits if val and val > 0)

        def run(e, fn=fn, waits=waits, inc=inc, inc_amt=inc_amt):
            for (s, val) in waits:
                e.wait_ge(s.h, val)
            inst = fn(e)
            if inc is not None:
                inst.then_inc(inc.h, inc_amt)

        self.p[eng].append(run)
        return v

    def raw(self, eng, fn):
        self.p[eng].append(fn)


def fm(t, chunk, c0, c1):
    return t[:, chunk * TOK + c0: chunk * TOK + c1]


def build():
    nc = bass.Bass("TRN2", target_bir_lowering=False)

    x_in = nc.dram_tensor("x", [TOK, DM], F32, kind="ExternalInput").ap()
    wqkv = nc.dram_tensor("wqkv", [DM, 3 * DM], F32, kind="ExternalInput").ap()
    wo = nc.dram_tensor("wo", [DM, DM], F32, kind="ExternalInput").ap()
    qwq = nc.dram_tensor("qw", [DM, 3072], I32, kind="ExternalInput").ap()
    qkvb_t = nc.dram_tensor("qkvb_t", [128, NOF], F32, kind="ExternalInput").ap()
    ob_t_in = nc.dram_tensor("ob_t", [128, NFC], F32, kind="ExternalInput").ap()
    fb_t_in = nc.dram_tensor("fb_t", [128, NFC], F32, kind="ExternalInput").ap()
    wsc_in = nc.dram_tensor("wsc_t", [128, NFC], F32, kind="ExternalInput").ap()
    wzp_in = nc.dram_tensor("wzp_t", [128, NFC], F32, kind="ExternalInput").ap()
    out = nc.dram_tensor("out", [TOK, DM], F32, kind="ExternalOutput").ap()

    q_scr = nc.dram_tensor("q_scr", [DM, TOK], BF16, kind="Internal").ap()
    kcc_in_t = nc.dram_tensor("kcc_in", [DM, TOK], BF16, kind="Internal")
    kcc_out_t = nc.dram_tensor("kcc_out", [2 * DM, TOK], BF16, kind="Internal")
    vcc_in_t = nc.dram_tensor("vcc_in", [512, 128, 128], BF16, kind="Internal")
    vcc_out_t = nc.dram_tensor("vcc_out", [1024, 128, 128], BF16, kind="Internal")
    kcc_in, kcc_out = kcc_in_t.ap(), kcc_out_t.ap()
    vcc_in, vcc_out = vcc_in_t.ap(), vcc_out_t.ap()
    attn_scr = nc.dram_tensor("attn_scr", [512, 128, 128], BF16, kind="Internal").ap()
    ffn_scr = nc.dram_tensor("ffn_scr", [512, 128, 128], BF16, kind="Internal").ap()
    x2_scr = nc.dram_tensor("x2_scr", [TOK, DM], F32, kind="Internal").ap()

    dbg = {}
    if DEBUG:
        for nm, shp, dt in [("d_q", [DM, TOK], BF16), ("d_k", [DM, TOK], BF16),
                            ("d_v", [512, 128, 128], BF16),
                            ("d_kg", [2 * DM, TOK], BF16),
                            ("d_ctx", [128, HEADS * TOK], BF16),
                            ("d_attn", [512, 128, 128], BF16),
                            ("d_x2", [TOK, DM], F32),
                            ("d_ffn", [512, 128, 128], BF16),
                            ("d_y1t", [128, 65536], BF16),
                            ("d_wnk0", [128, DM], BF16),
                            ("d_mv0", [128, 8], F32)]:
            dbg[nm] = nc.dram_tensor(nm, shp, dt, kind="ExternalOutput").ap()

    with contextlib.ExitStack() as st:
        def sb(name, shape, dt=F32):
            return st.enter_context(nc.sbuf_tensor(name, list(shape), dt))

        def ps(name, shape, dt=F32):
            return st.enter_context(nc.psum_tensor(name, list(shape), dt))

        big = sb("big", [128, 65536], BF16)
        dbuf = sb("dbuf", [128, 8192], BF16)
        bufC = sb("bufC", [128, DM], F32)
        bufD = sb("bufD", [128, DM], F32)
        stg = sb("stg", [128, DM], BF16)
        wnk = sb("wnk", [128, DM], BF16)
        wstg = sb("wstg", [128, 512], F32)
        pbuf = sb("pbuf", [128, 2048], BF16)
        qbuf = sb("qbuf", [128, 1024], BF16)
        bcast = sb("bcast", [128, 512], F32)
        eye = sb("eye", [128, 128], BF16)
        bqkv = sb("bqkv", [128, NOF], F32)
        bob = sb("bob", [128, NFC], F32)
        bfb = sb("bfb", [128, NFC], F32)
        bsc = sb("bsc", [128, NFC], F32)
        bzp = sb("bzp", [128, NFC], F32)
        stt = sb("stt", [128, 64], F32)
        mv = sb("mv", [128, 8], F32)
        cst = sb("cst", [128, 8], F32)
        ones = sb("ones", [128, 1], BF16)
        onesr = sb("onesr", [1, 128], BF16)
        rrow = sb("rrow", [1, 512], BF16)
        rrowf = sb("rrowf", [1, 512], F32)
        cnf = sb("cnf", [128, 16], BF16)

        pA = ps("pA", [128, 1024])
        pB = ps("pB", [128, 1024])
        pC = ps("pC", [128, 512])
        pD = ps("pD", [128, 512])
        pE = ps("pE", [128, 512])
        pF = ps("pF", [128, 512])

        sL = Sem(st.enter_context(nc.semaphore("sL")))
        sS = Sem(st.enter_context(nc.semaphore("sS")))
        sG = Sem(st.enter_context(nc.semaphore("sG")))
        sCC = Sem(st.enter_context(nc.semaphore("sCC")))
        sPE = Sem(st.enter_context(nc.semaphore("sPE")))
        sVE = Sem(st.enter_context(nc.semaphore("sVE")))
        sAE = Sem(st.enter_context(nc.semaphore("sAE")))
        sPO = Sem(st.enter_context(nc.semaphore("sPO")))

        P = Prog()

        def echo(queue, sem):
            return P.op(queue, lambda e: e.dma_start(out=stt[0:1, 48:64],
                                                     in_=x_in[0:1, 0:16]),
                        inc=sem, inc_amt=16)

        consts = [0.0, 1.0, SM_SCALE, ESHIFT]
        for i, cv in enumerate(consts):
            P.op("gpsimd", (lambda i=i, cv=cv: (lambda e: e.memset(cst[:, i:i + 1], cv)))())
            nc.const_aps.register(F32, cv, cst[:, i:i + 1])
        P.op("gpsimd", lambda e: e.memset(ones[:], 1.0))
        P.op("gpsimd", lambda e: e.memset(onesr[:], 1.0))
        P.op("gpsimd", lambda e: e.memset(eye[:], 0.0))
        setup_done = P.op("gpsimd", lambda e: e.affine_select(
            out=eye[:], in_=eye[:], compare_op=ALU.not_equal, fill=1.0,
            base=0, pattern=[[-1, 128]], channel_multiplier=1), inc=sPO)

        for dst, src in [(bqkv, qkvb_t), (bob, ob_t_in), (bfb, fb_t_in),
                         (bsc, wsc_in), (bzp, wzp_in)]:
            P.op("sync", (lambda dst=dst, src=src: (lambda e: e.dma_start(out=dst[:], in_=src)))(),
                 inc=sL, inc_amt=16)
        bias_ld = echo("sync", sL)

        def emit_ln_stats_apply(xsrc, ydst, ready_waits, apply_extra_waits=()):
            vlast = None
            for i in range(8):
                w = ready_waits if i == 0 else ()
                vlast = P.op("vector", (lambda i=i, xsrc=xsrc: (lambda e: e.bn_stats(
                    stt[:, i * 6:(i + 1) * 6], xsrc[:, i * 512:(i + 1) * 512])))(),
                    waits=w, inc=sVE)
            vlast = P.op("vector", lambda e: e.bn_aggr(mv[:, 0:2], stt[:, 0:48]),
                         waits=[(sVE, vlast)], inc=sVE)
            ve = P.op("vector", lambda e: e.tensor_single_scalar(mv[:, 4:5], mv[:, 1:2], EPS, ALU.add),
                      waits=[(sVE, vlast)], inc=sVE)
            ae = P.op("scalar", lambda e: e.activation(mv[:, 5:6], mv[:, 4:5], AF.Sqrt,
                                                       bias=0.0, scale=1.0),
                      waits=[(sVE, ve), (sPO, setup_done)], inc=sAE)
            vr = P.op("vector", lambda e: e.reciprocal(mv[:, 2:3], mv[:, 5:6]),
                      waits=[(sAE, ae)], inc=sVE)
            ve = P.op("vector", lambda e: e.tensor_scalar(mv[:, 3:4], mv[:, 0:1],
                                                          mv[:, 2:3], -1.0, ALU.mult, ALU.mult),
                      waits=[(sVE, vr)], inc=sVE)
            ae = P.op("scalar", (lambda xsrc=xsrc, ydst=ydst: (lambda e: e.activation(
                ydst[:], xsrc[:], AF.Identity, bias=mv[:, 3:4], scale=mv[:, 2:3])))(),
                waits=[(sVE, ve)] + list(apply_extra_waits), inc=sAE)
            return ae

        tr_state = {"last_evac": {}, "mm": None}

        def emit_transpose_to_big(ysrc, tt, first_waits):
            last_pe = None
            last_ve = None
            for g in range(8):
                bank = pC if g % 2 == 0 else pD
                for j in range(4):
                    fc = g * 4 + j
                    w = []
                    if fc == 0:
                        w += list(first_waits)
                    if j == 0:
                        prev = tr_state["last_evac"].get(g % 2)
                        if prev:
                            w.append((sVE, prev))
                    inc = sPE if j == 3 else None
                    r = P.op("tensor", (lambda bank=bank, j=j, fc=fc, ysrc=ysrc: (lambda e: e.matmul(
                        bank[:, j * 128:(j + 1) * 128], ysrc[:, fc * 128:(fc + 1) * 128],
                        eye[:], start=True, stop=True)))(), waits=w, inc=inc)
                    if inc:
                        last_pe = r
                def mkevac(bank=bank, g=g, tt=tt):
                    def f(e):
                        dst = big[:].rearrange("p (f t) -> p f t", t=TOK)[
                            :, g * 4:(g + 1) * 4, tt * 128:(tt + 1) * 128]
                        src = bank[:].rearrange("p (f j) -> p f j", j=128)
                        return e.tensor_copy(dst, src)
                    return f
                last_ve = P.op("vector", mkevac(), waits=[(sPE, last_pe)], inc=sVE)
                tr_state["last_evac"][g % 2] = last_ve
            last_ve = P.op("scalar", (lambda tt=tt: (lambda e: e.activation(
                big[:].rearrange("p (f t) -> p f t", t=TOK)[:, :, tt * 128:(tt + 1) * 128],
                big[:].rearrange("p (f t) -> p f t", t=TOK)[:, :, tt * 128:(tt + 1) * 128],
                AF.Identity, bias=0.0, scale=1.0)))(),
                waits=[(sVE, last_ve), (sPO, setup_done)], inc=sVE)
            return last_pe, last_ve

        p1_apply = {}
        p1_ve_done = None
        for tt in range(NT):
            xb = bufC if tt % 2 == 0 else bufD
            w = [(sL, bias_ld)] if tt == 0 else []
            if tt >= 2:
                w.append((sAE, p1_apply[tt - 2]))
            xl = P.op("sync", (lambda tt=tt, xb=xb: (lambda e: e.dma_start(
                out=xb[:], in_=x_in[tt * 128:(tt + 1) * 128, :])))(),
                waits=w, inc=sL, inc_amt=16)
            xl = echo("sync", sL)
            aw = [(sPE, tr_state["mm"])] if tr_state["mm"] else []
            ae = emit_ln_stats_apply(xb, wnk, ready_waits=[(sL, xl)], apply_extra_waits=aw)
            p1_apply[tt] = ae
            if DEBUG and tt == 0:
                P.op("scalar", lambda e: e.dma_start(out=dbg["d_wnk0"], in_=wnk[:]),
                     waits=[(sAE, ae)], inc=sS, inc_amt=16)
                P.op("scalar", lambda e: e.dma_start(out=dbg["d_mv0"], in_=mv[:]),
                     waits=[(sAE, ae)], inc=sS, inc_amt=16)
            mmn, p1_ve_done = emit_transpose_to_big(wnk, tt, first_waits=[(sAE, ae)])
            tr_state["mm"] = mmn
        p1_done_ve = p1_ve_done
        if DEBUG:
            P.op("scalar", lambda e: e.dma_start(out=dbg["d_y1t"], in_=big[:]),
                 waits=[(sVE, p1_done_ve)], inc=sS, inc_amt=16)

        BANKS8 = [pA[:, 0:512], pA[:, 512:1024], pB[:, 0:512], pB[:, 512:1024],
                  pC[:, :], pD[:, :], pE[:, :], pF[:, :]]
        BANKS4 = BANKS8[:4]

        wsn = {"cast": {}}
        fcuse = {}
        evac_ae = {}
        stg_store = {}
        gidx_counter = [0]
        wdma = {}

        def weight_stream(wt, col0, gidx, fc, cast_extra=()):
            slot = (gidx * 32 + fc) % 2
            w = []
            prevc = wsn["cast"].get((gidx * 32 + fc) - 2)
            if prevc:
                w.append((sPO, prevc))
            ld = P.op("sync", (lambda wt=wt, col0=col0, fc=fc, slot=slot: (lambda e: e.dma_start(
                out=wstg[:, slot * 256:(slot + 1) * 256],
                in_=wt[fc * 128:(fc + 1) * 128, col0:col0 + 256])))(),
                waits=w, inc=sL, inc_amt=16)
            wdma[(gidx, fc)] = ld
            w2 = [(sL, ld + 16)] + list(cast_extra)
            pu = fcuse.get((gidx - 1, fc))
            if pu:
                w2.append((sPE, pu))
            c = P.op("scalar", (lambda fc=fc, slot=slot: (lambda e: e.activation(
                dbuf[:, fc * 256:(fc + 1) * 256], wstg[:, slot * 256:(slot + 1) * 256],
                AF.Identity, bias=0.0, scale=1.0)))(),
                waits=w2 + [(sPO, setup_done)], inc=sPO)
            wsn["cast"][gidx * 32 + fc] = c

        def qkv_group(of0, kind, extra_first_waits=(), cast_extra=(), last_in_segment=False):
            gidx = gidx_counter[0]
            gidx_counter[0] += 1
            wt = wo if kind == "wo" else wqkv
            col0 = of0 * 128
            for fc in range(NFC):
                weight_stream(wt, col0, gidx, fc, cast_extra=cast_extra)
            if last_in_segment:
                echo("sync", sL)
            act_src = big

            def mm_of_fc(fc, j, sl, bank, start, stop, extra_w=(), inc=None):
                return P.op("tensor", (lambda fc=fc, j=j, sl=sl, bank=bank, start=start, stop=stop:
                            (lambda e: e.matmul(bank, dbuf[:, fc * 256 + j * 128: fc * 256 + (j + 1) * 128],
                                                fm(act_src, fc, sl * 512, (sl + 1) * 512),
                                                start=start, stop=stop)))(),
                            waits=list(extra_w), inc=inc)

            if kind in ("k", "q"):
                passes = [(0, 1, 2, 3)]
                banksel = lambda j, si: BANKS8[j * 4 + si]
            else:
                passes = [(0, 1), (2, 3)]
                banksel = lambda j, si: BANKS4[j * 2 + si]

            for pi, sls in enumerate(passes):
                for fc in range(NFC):
                    first_extra = []
                    if pi == 0:
                        first_extra.append((sPO, wsn["cast"][gidx * 32 + fc]))
                    if fc == 0:
                        if pi > 0:
                            pe_prev = evac_ae.get((gidx, pi - 1))
                        else:
                            pe_prev = evac_ae.get((gidx - 1, "last"))
                        if pe_prev:
                            first_extra.append((sAE, pe_prev))
                        if gidx == 0:
                            first_extra.append((sVE, p1_done_ve))
                        first_extra.extend(extra_first_waits)
                        extra_first_waits = ()
                    nmm = len(sls) * 2
                    cnt = 0
                    for j in range(2):
                        for si, sl in enumerate(sls):
                            cnt += 1
                            inc = sPE if cnt == nmm else None
                            r = mm_of_fc(fc, j, sl, banksel(j, si),
                                         start=(fc == 0), stop=(fc == NFC - 1),
                                         extra_w=first_extra if cnt == 1 else (), inc=inc)
                    fcuse[(gidx, fc)] = r
                last_pe = fcuse[(gidx, NFC - 1)]
                for j in range(2):
                    for si, sl in enumerate(sls):
                        of = of0 + j
                        slotid = (j * len(sls) + si) if kind in ("v", "wo") else (j * 4 + si)
                        w = [(sPE, last_pe)]
                        pv = stg_store.get(slotid)
                        if pv:
                            w.append((sS, pv))
                        bias = bqkv[:, of:of + 1] if kind != "wo" else bob[:, of:of + 1]
                        ae = P.op("scalar", (lambda bank=banksel(j, si), slotid=slotid, bias=bias:
                                  (lambda e: e.activation(stg[:, slotid * 512:(slotid + 1) * 512],
                                                          bank, AF.Identity, bias=bias, scale=1.0)))(),
                                  waits=w, inc=sAE)
                        if kind in ("k", "q"):
                            base = q_scr if kind == "q" else kcc_in
                            r0 = (of - (32 if kind == "k" else 0)) * 128
                            sc = P.op("scalar", (lambda base=base, r0=r0, sl=sl, slotid=slotid:
                                      (lambda e: e.dma_start(
                                          out=base[r0:r0 + 128, sl * 512:(sl + 1) * 512],
                                          in_=stg[:, slotid * 512:(slotid + 1) * 512])))(),
                                      waits=[(sAE, ae)], inc=sS, inc_amt=16)
                            stg_store[slotid] = sc
                        else:
                            tb = pE if (j + si) % 2 == 0 else pF
                            tslot = 4 + (j * len(sls) + si) % 4
                            tpe = None
                            for i4 in range(4):
                                wv = [(sAE, ae)] if i4 == 0 else []
                                pvv = tr_state["last_evac"].get(("p2t", (j + si) % 2))
                                if i4 == 0 and pvv:
                                    wv.append((sVE, pvv))
                                tpe = P.op("tensor", (lambda tb=tb, i4=i4, slotid=slotid:
                                           (lambda e: e.matmul(tb[:, i4 * 128:(i4 + 1) * 128],
                                                               stg[:, slotid * 512 + i4 * 128: slotid * 512 + (i4 + 1) * 128],
                                                               eye[:], start=True, stop=True)))(),
                                           waits=wv, inc=(sPE if i4 == 3 else None))
                            wv2 = [(sPE, tpe)]
                            pv2 = stg_store.get(tslot)
                            if pv2:
                                wv2.append((sS, pv2))
                            ve = P.op("vector", (lambda tb=tb, tslot=tslot: (lambda e: e.tensor_copy(
                                stg[:, tslot * 512:(tslot + 1) * 512], tb[:])))(),
                                waits=wv2, inc=sVE)
                            tr_state["last_evac"][("p2t", (j + si) % 2)] = ve
                            for i4 in range(4):
                                tt = sl * 4 + i4
                                if kind == "v":
                                    blk = vcc_in[(of - 64) * 16 + tt]
                                else:
                                    blk = attn_scr[tt * 32 + of]
                                sc = P.op("scalar", (lambda blk=blk, tslot=tslot, i4=i4:
                                          (lambda e: e.dma_start(
                                              out=blk,
                                              in_=stg[:, tslot * 512 + i4 * 128: tslot * 512 + (i4 + 1) * 128])))(),
                                          waits=[(sVE, ve)], inc=sS, inc_amt=16)
                            stg_store[tslot] = sc
                evac_ae[(gidx, pi)] = sAE.n
            evac_ae[(gidx, "last")] = sAE.n
            return gidx

        for gof in range(16):
            qkv_group(64 + gof * 2, "v")
        v_stores_done = sS.n
        cc_v = None
        for gof in range(16):
            qkv_group(32 + gof * 2, "k")
        k_stores_done = sS.n
        cc_k = None
        for gof in range(16):
            qkv_group(gof * 2, "q", last_in_segment=(gof == 15))
        q_stores_done = sS.n
        cc_v = P.op("gpsimd", lambda e: e.collective_compute(
            "AllGather", ALU.bypass,
            replica_groups=[[0, 1], [2, 3], [4, 5], [6, 7]],
            ins=[vcc_in_t.ap().opt()], outs=[vcc_out_t.ap().opt()]),
            waits=[(sS, q_stores_done)], inc=sCC)
        cc_k = P.op("gpsimd", lambda e: e.collective_compute(
            "AllGather", ALU.bypass,
            replica_groups=[[0, 1], [2, 3], [4, 5], [6, 7]],
            ins=[kcc_in_t.ap().opt()], outs=[kcc_out_t.ap().opt()]),
            waits=[(sS, q_stores_done)], inc=sCC)
        p2_last_pe = sPE.n
        p2_last_ae = sAE.n

        if DEBUG:
            P.op("sync", lambda e: e.dma_start(out=dbg["d_q"], in_=q_scr),
                 waits=[(sS, q_stores_done)], inc=sL, inc_amt=16)
            P.op("sync", lambda e: e.dma_start(out=dbg["d_k"], in_=kcc_in),
                 waits=[(sS, k_stores_done)], inc=sL, inc_amt=16)
            P.op("sync", lambda e: e.dma_start(out=dbg["d_v"], in_=vcc_in),
                 waits=[(sS, v_stores_done)], inc=sL, inc_amt=16)
            P.op("sync", lambda e: e.dma_start(out=dbg["d_kg"], in_=kcc_out),
                 waits=[(sCC, cc_k)], inc=sL, inc_amt=16)

        kv_ld = {}
        s_done = {}
        exp_done = {}
        ctx_done = {}
        smm_last = {}
        ctx_evac = {}
        head_last_mm = {}
        bc_copy = {}
        qld = {}

        for h in range(HEADS):
            w = [(sCC, cc_k), (sS, q_stores_done)]
            if h == 0:
                w.append((sPE, p2_last_pe))
            else:
                w.append((sPE, head_last_mm[h - 1]))
            for r in range(2):
                P.op("sync", (lambda h=h, r=r: (lambda e: e.dma_start(
                    out=dbuf[:, r * 2048:(r + 1) * 2048],
                    in_=kcc_out[r * DM + h * 128: r * DM + (h + 1) * 128, :])))(),
                    waits=w if r == 0 else [], inc=sL, inc_amt=16)
            for r in range(2):
                P.op("sync", (lambda h=h, r=r: (lambda e: e.dma_start(
                    out=dbuf[:, 4096 + r * 2048: 4096 + (r + 1) * 2048].rearrange(
                        "p (t j) -> p t j", j=128),
                    in_=vcc_out[r * 512 + h * 16: r * 512 + (h + 1) * 16].rearrange(
                        "t p j -> p t j"))))(),
                    waits=[(sCC, cc_v)] if r == 0 else [], inc=sL, inc_amt=16)
            kv_ld[h] = echo("sync", sL)

            for qt in range(4):
                i = h * 4 + qt
                wq = []
                if i >= 2 and smm_last.get(i - 2):
                    wq.append((sPE, smm_last[i - 2]))
                P.op("sync", (lambda h=h, qt=qt, i=i: (lambda e: e.dma_start(
                    out=qbuf[:, (i % 2) * 512:(i % 2) * 512 + 512],
                    in_=q_scr[h * 128:(h + 1) * 128, qt * 512:(qt + 1) * 512])))(),
                    waits=wq, inc=sL, inc_amt=16)
                qld[i] = echo("sync", sL)

                ctxbank = pC if i % 2 == 0 else pD
                lbank = pE if i % 2 == 0 else pF
                for jp in range(16):
                    par = jp % 2
                    sb_ = pA if par == 0 else pB
                    gpair = i * 16 + jp
                    w1 = []
                    pe_prev = exp_done.get(gpair - 2)
                    if pe_prev:
                        w1.append((sAE, pe_prev))
                    if jp == 0:
                        w1 += [(sL, qld[i]), (sL, kv_ld[h])]
                        pv = ctx_evac.get(i - 2)
                        if pv:
                            w1.append((sVE, pv))
                        pv = bc_copy.get(i - 1)
                        if pv:
                            w1.append((sVE, pv))
                        if i == 0:
                            w1.append((sPO, setup_done))
                    for half in range(2):
                        kc = jp * 2 + half
                        rsel, idx = kc // 16, kc % 16
                        r = P.op("tensor", (lambda sb_=sb_, half=half, rsel=rsel, idx=idx, i=i:
                                 (lambda e: e.matmul(
                                     sb_[:, half * 512:(half + 1) * 512],
                                     dbuf[:, rsel * 2048 + idx * 128: rsel * 2048 + (idx + 1) * 128],
                                     qbuf[:, (i % 2) * 512:(i % 2) * 512 + 512],
                                     start=True, stop=True)))(),
                                 waits=w1 if half == 0 else (), inc=(sPE if half == 1 else None))
                    s_done[gpair] = r
                    smm_last[i] = r
                    w2 = [(sPE, s_done[gpair])]
                    pv = ctx_done.get(gpair - 2)
                    if pv:
                        w2.append((sPE, pv))
                    exp_done[gpair] = P.op("scalar", (lambda sb_=sb_, par=par: (lambda e: e.activation(
                        pbuf[:, par * 1024:(par + 1) * 1024], sb_[:],
                        AF.Exp, bias=ESHIFT, scale=SM_SCALE)))(),
                        waits=w2, inc=sAE)
                    w3 = [(sAE, exp_done[gpair])]
                    lastmm = None
                    for half in range(2):
                        kc = jp * 2 + half
                        rsel, idx = kc // 16, kc % 16
                        P.op("tensor", (lambda ctxbank=ctxbank, par=par, half=half, rsel=rsel, idx=idx, kc=kc:
                             (lambda e: e.matmul(
                                 ctxbank[:], dbuf[:, 4096 + rsel * 2048 + idx * 128: 4096 + rsel * 2048 + (idx + 1) * 128],
                                 pbuf[:, par * 1024 + half * 512: par * 1024 + (half + 1) * 512],
                                 start=(kc == 0), stop=(kc == 31))))(),
                             waits=w3 if half == 0 else ())
                    for half in range(2):
                        kc = jp * 2 + half
                        lastmm = P.op("tensor", (lambda lbank=lbank, par=par, half=half, kc=kc:
                                      (lambda e: e.matmul(
                                          lbank[0:1, :], ones[:],
                                          pbuf[:, par * 1024 + half * 512: par * 1024 + (half + 1) * 512],
                                          start=(kc == 0), stop=(kc == 31))))(),
                                      inc=(sPE if half == 1 else None))
                    ctx_done[gpair] = lastmm
                vr = P.op("vector", (lambda lbank=lbank: (lambda e: e.reciprocal(
                    rrowf[:], lbank[0:1, :])))(), waits=[(sPE, ctx_done[i * 16 + 15])], inc=sVE)
                ve1 = P.op("scalar", lambda e: e.activation(rrow[:], rrowf[:],
                           AF.Identity, bias=0.0, scale=1.0),
                           waits=[(sVE, vr)], inc=sVE)
                wb = [(sVE, ve1), (sAE, exp_done[i * 16 + 14])]
                bpe = P.op("tensor", lambda e: e.matmul(
                    pA[:, 0:512], onesr[:], rrow[:], start=True, stop=True),
                    waits=wb, inc=sPE)
                ve2 = P.op("vector", lambda e: e.tensor_copy(bcast[:], pA[:, 0:512]),
                           waits=[(sPE, bpe)], inc=sVE)
                bc_copy[i] = ve2
                cx = P.op("vector", (lambda ctxbank=ctxbank, h=h, qt=qt: (lambda e: e.tensor_mul(
                    big[:, h * TOK + qt * 512: h * TOK + (qt + 1) * 512],
                    ctxbank[:], bcast[:])))(), waits=[(sVE, ve2)], inc=sVE)
                ctx_evac[i] = P.op("vector", (lambda h=h, qt=qt: (lambda e: e.tensor_copy(
                    cnf[:], big[:, h * TOK + qt * 512 + 496: h * TOK + qt * 512 + 512])))(),
                    waits=[(sVE, cx)], inc=sVE)
                head_last_mm[h] = ctx_done[i * 16 + 15]
        p3_done_ve = sVE.n
        p3_done_pe = sPE.n

        if DEBUG:
            P.op("scalar", lambda e: e.dma_start(out=dbg["d_ctx"], in_=big[:]),
                 waits=[(sVE, p3_done_ve)], inc=sS, inc_amt=16)

        p3_seal = P.op("scalar", lambda e: e.activation(
            big[:, 31 * TOK + 3 * 512: 31 * TOK + 4 * 512],
            big[:, 31 * TOK + 3 * 512: 31 * TOK + 4 * 512],
            AF.Identity, bias=0.0, scale=1.0), waits=[(sVE, sVE.n)], inc=sAE)
        p3_exp_last = sAE.n
        for gof in range(16):
            qkv_group(gof * 2, "wo",
                      extra_first_waits=((sVE, p3_done_ve), (sAE, p3_exp_last)) if gof == 0 else (),
                      cast_extra=((sPE, p3_done_pe),) if gof == 0 else (),
                      last_in_segment=(gof == 15))
        p4_stores_done = sS.n
        p4_last_pe = sPE.n

        if DEBUG:
            P.op("sync", lambda e: e.dma_start(out=dbg["d_attn"], in_=attn_scr),
                 waits=[(sS, p4_stores_done)], inc=sL, inc_amt=16)

        x2_store = {}
        ln2_ae = {}
        ln3_ae = {}
        add_ve = {}
        for tt in range(NT):
            wa = [(sS, p4_stores_done)] if tt == 0 else [(sAE, ln2_ae[tt - 1])]
            al = P.op("sync", (lambda tt=tt: (lambda e: e.dma_start(
                out=stg[:].rearrange("p (t j) -> p t j", j=128),
                in_=attn_scr[tt * 32:(tt + 1) * 32].rearrange("t p j -> p t j"))))(),
                waits=wa, inc=sL, inc_amt=16)
            al = echo("sync", sL)
            wx = []
            if tt >= 1:
                wx.append((sAE, ln3_ae[tt - 1]))
                wx.append((sS, x2_store[tt - 1]))
            xl = P.op("sync", (lambda tt=tt: (lambda e: e.dma_start(
                out=bufC[:], in_=x_in[tt * 128:(tt + 1) * 128, :])))(),
                waits=wx, inc=sL, inc_amt=16)
            xl = echo("sync", sL)
            wd = [(sVE, add_ve[tt - 1])] if tt >= 1 else []
            ln2_ae[tt] = emit_ln_stats_apply(stg, bufD, ready_waits=[(sL, al)],
                                             apply_extra_waits=wd)
            add_ve[tt] = P.op("vector", lambda e: e.tensor_add(bufC[:], bufC[:], bufD[:]),
                              waits=[(sAE, ln2_ae[tt]), (sL, xl)], inc=sVE)
            x2_store[tt] = P.op("scalar", (lambda tt=tt: (lambda e: e.dma_start(
                out=x2_scr[tt * 128:(tt + 1) * 128, :], in_=bufC[:])))(),
                waits=[(sVE, add_ve[tt])], inc=sS, inc_amt=16)
            aw = [(sPE, tr_state["mm"])] if tr_state["mm"] else []
            ln3_ae[tt] = emit_ln_stats_apply(bufC, wnk, ready_waits=[(sVE, add_ve[tt])],
                                             apply_extra_waits=aw)
            first = [(sAE, ln3_ae[tt])]
            if tt == 0:
                first.append((sPE, p4_last_pe))
            mmn, _ = emit_transpose_to_big(wnk, tt, first_waits=first)
            tr_state["mm"] = mmn
        p5_done_ve = sVE.n
        p5_last_ae = sAE.n
        x2_all = sS.n

        if DEBUG:
            P.op("scalar", lambda e: e.dma_start(out=dbg["d_x2"], in_=x2_scr),
                 waits=[(sS, x2_all)], inc=sS, inc_amt=16)

        qw_ld = {}
        deq_done = {}
        wT_evac = {}
        ffn_mm_done = {}
        ffn_stores = {}
        for n in range(NFC):
            w = []
            if n == 0:
                w = [(sS, x2_all), (sAE, p5_last_ae), (sVE, p5_done_ve)]
            else:
                w = [(sVE, deq_done[n - 1])]
            P.op("gpsimd", (lambda n=n: (lambda e: e.dma_start(
                out=bufC[:, 0:3072], in_=qwq[n * 128:(n + 1) * 128, :])))(),
                waits=w, inc=sG, inc_amt=16)
            qw_ld[n] = echo("gpsimd", sG)
            wv = [(sG, qw_ld[n])]
            if tr_state["mm"]:
                wv.append((sPE, tr_state["mm"]))
            P.op("vector", lambda e: e.tensor_scalar(
                bufD[:, 0:3072], bufC[:, 0:3072], 1.0 / 64.0, -0.4921875,
                ALU.mult, ALU.add), waits=wv)
            P.op("vector", lambda e: e.tensor_scalar(
                bufD[:, 0:3072], bufD[:, 0:3072], C_ROUND, -C_ROUND,
                ALU.add, ALU.add))
            P.op("vector", lambda e: e.scalar_tensor_tensor(
                bufC[:, 0:3072], bufD[:, 0:3072], -64.0, bufC[:, 0:3072],
                ALU.mult, ALU.add))

            vm = P.op("vector", (lambda n=n: (lambda e: e.tensor_scalar(
                mv[:, 6:7], bzp[:, n:n + 1], bsc[:, n:n + 1], -1.0,
                ALU.mult, ALU.mult)))(), inc=sVE)

            def wv_(src, j, outj, n=n):
                sv = src[:, 0:3072].rearrange("p (g w) -> p g w", w=3)[:, :, j]
                ov = wnk[:, 0:DM].rearrange("p (g f) -> p g f", f=4)[:, :, outj]
                def f(e):
                    return e.activation(ov, sv, AF.Identity,
                                        bias=mv[:, 6:7], scale=bsc[:, n:n + 1])
                return f
            P.op("scalar", wv_(bufC, 0, 0), waits=[(sVE, vm)])
            P.op("scalar", wv_(bufD, 0, 1))
            P.op("scalar", wv_(bufC, 1, 2))
            deq_done[n] = P.op("scalar", wv_(bufC, 2, 3), inc=sVE)

            tslot = n % 2
            last_pe = None
            for g in range(8):
                bank = pC if g % 2 == 0 else pD
                for j in range(4):
                    fc = g * 4 + j
                    w = []
                    if fc == 0:
                        w.append((sVE, deq_done[n]))
                        if n >= 2 and ffn_mm_done.get(n - 2):
                            w.append((sPE, ffn_mm_done[n - 2]))
                        if n == 0:
                            w.append((sPE, p4_last_pe))
                    if j == 0:
                        pv = tr_state["last_evac"].get(("p6", g % 2))
                        if pv:
                            w.append((sVE, pv))
                    last_pe = P.op("tensor", (lambda bank=bank, j=j, fc=fc: (lambda e: e.matmul(
                        bank[:, j * 128:(j + 1) * 128], wnk[:, fc * 128:(fc + 1) * 128],
                        eye[:], start=True, stop=True)))(),
                        waits=w, inc=(sPE if j == 3 else None))
                ve = P.op("scalar", (lambda bank=bank, g=g, tslot=tslot: (lambda e: e.activation(
                    dbuf[:, tslot * 4096 + g * 512:tslot * 4096 + (g + 1) * 512], bank[:],
                    AF.Identity, bias=0.0, scale=1.0)))(),
                    waits=[(sPE, last_pe)], inc=sVE)
                tr_state["last_evac"][("p6", g % 2)] = ve
            wT_evac[n] = ve
            tr_state["mm"] = last_pe

            last = None
            for kc in range(NFC):
                for sl in range(4):
                    w = []
                    if kc == 0 and sl == 0:
                        w.append((sVE, wT_evac[n]))
                        pv = ffn_stores.get(n - 1)
                        if pv:
                            w.append((sAE, ffn_stores[n - 1][0]))
                    last = P.op("tensor", (lambda kc=kc, sl=sl, tslot=tslot: (lambda e: e.matmul(
                        BANKS4[sl], dbuf[:, tslot * 4096 + kc * 128: tslot * 4096 + (kc + 1) * 128],
                        fm(big, kc, sl * 512, (sl + 1) * 512),
                        start=(kc == 0), stop=(kc == NFC - 1))))(),
                        waits=w, inc=(sPE if (kc == NFC - 1 and sl == 3) else None))
            ffn_mm_done[n] = last

            ae_last = None
            for sl in range(4):
                w = [(sPE, ffn_mm_done[n])]
                pv = stg_store.get(("p6", sl))
                if pv:
                    w.append((sS, pv))
                ae = P.op("scalar", (lambda sl=sl, n=n: (lambda e: e.activation(
                    stg[:, sl * 512:(sl + 1) * 512], BANKS4[sl], AF.Identity,
                    bias=bfb[:, n:n + 1], scale=1.0)))(), waits=w, inc=sAE)
                tb = pE if sl % 2 == 0 else pF
                tpe = None
                for i4 in range(4):
                    w2 = [(sAE, ae)] if i4 == 0 else []
                    pv2 = tr_state["last_evac"].get(("p6o", sl % 2))
                    if i4 == 0 and pv2:
                        w2.append((sVE, pv2))
                    tpe = P.op("tensor", (lambda tb=tb, i4=i4, sl=sl: (lambda e: e.matmul(
                        tb[:, i4 * 128:(i4 + 1) * 128],
                        stg[:, sl * 512 + i4 * 128: sl * 512 + (i4 + 1) * 128],
                        eye[:], start=True, stop=True)))(),
                        waits=w2, inc=(sPE if i4 == 3 else None))
                w3 = [(sPE, tpe)]
                pv3 = stg_store.get(("p6b", sl))
                if pv3:
                    w3.append((sS, pv3))
                ve = P.op("vector", (lambda tb=tb, sl=sl: (lambda e: e.tensor_copy(
                    stg[:, (4 + sl) * 512:(5 + sl) * 512], tb[:])))(),
                    waits=w3, inc=sVE)
                tr_state["last_evac"][("p6o", sl % 2)] = ve
                sc = None
                for i4 in range(4):
                    tt = sl * 4 + i4
                    sc = P.op("scalar", (lambda tt=tt, n=n, sl=sl, i4=i4: (lambda e: e.dma_start(
                        out=ffn_scr[tt * 32 + n],
                        in_=stg[:, (4 + sl) * 512 + i4 * 128: (4 + sl) * 512 + (i4 + 1) * 128])))(),
                        waits=[(sVE, ve)], inc=sS, inc_amt=16)
                stg_store[("p6", sl)] = sS.n
                stg_store[("p6b", sl)] = sc
                ae_last = ae
            ffn_stores[n] = (ae_last, sS.n)
        p6_stores_done = sS.n

        if DEBUG:
            P.op("sync", lambda e: e.dma_start(out=dbg["d_ffn"], in_=ffn_scr),
                 waits=[(sS, p6_stores_done)], inc=sL, inc_amt=16)

        ln4_ae = {}
        out_store = {}
        add7 = {}
        for tt in range(NT):
            wa = [(sS, p6_stores_done)] if tt == 0 else [(sAE, ln4_ae[tt - 1])]
            fl = P.op("sync", (lambda tt=tt: (lambda e: e.dma_start(
                out=stg[:].rearrange("p (t j) -> p t j", j=128),
                in_=ffn_scr[tt * 32:(tt + 1) * 32].rearrange("t p j -> p t j"))))(),
                waits=wa, inc=sL, inc_amt=16)
            fl = echo("sync", sL)
            wx = []
            if tt >= 1:
                wx.append((sVE, add7[tt - 1]))
            xl = P.op("sync", (lambda tt=tt: (lambda e: e.dma_start(
                out=bufC[:], in_=x2_scr[tt * 128:(tt + 1) * 128, :])))(),
                waits=wx, inc=sL, inc_amt=16)
            xl = echo("sync", sL)
            wd = [(sS, out_store[tt - 1])] if tt >= 1 else []
            ln4_ae[tt] = emit_ln_stats_apply(stg, bufD, ready_waits=[(sL, fl)],
                                             apply_extra_waits=wd)
            add7[tt] = P.op("vector", lambda e: e.tensor_add(bufD[:], bufC[:], bufD[:]),
                            waits=[(sAE, ln4_ae[tt]), (sL, xl)], inc=sVE)
            out_store[tt] = P.op("scalar", (lambda tt=tt: (lambda e: e.dma_start(
                out=out[tt * 128:(tt + 1) * 128, :], in_=bufD[:])))(),
                waits=[(sVE, add7[tt])], inc=sS, inc_amt=16)

        final_sS = sS.n
        P.raw("sync", lambda e: e.wait_ge(sS.h, final_sS))
        P.raw("gpsimd", lambda e: e.wait_ge(sS.h, final_sS))

        with nc.Block() as block:
            @block.sync
            def _(e):
                for f in P.p["sync"]:
                    f(e)

            @block.gpsimd
            def _(e):
                for f in P.p["gpsimd"]:
                    f(e)

            @block.tensor
            def _(e):
                for f in P.p["tensor"]:
                    f(e)

            @block.vector
            def _(e):
                for f in P.p["vector"]:
                    f(e)

            @block.scalar
            def _(e):
                for f in P.p["scalar"]:
                    f(e)
    return nc


_NC_CACHE = [None]


def kernel(L0_attn_x, L0_attn_attn_qkvw, L0_attn_attn_qkvb, L0_attn_attn_ow,
           L0_attn_attn_ob, L0_ffn_qweight, L0_ffn_weight_scale,
           L0_ffn_weight_zero_point, L0_ffn_bias):
    kernel._LAST = None
    x = np.ascontiguousarray(np.asarray(L0_attn_x, dtype=np.float32))
    wqkv = np.ascontiguousarray(np.asarray(L0_attn_attn_qkvw, dtype=np.float32))
    qkvb = np.asarray(L0_attn_attn_qkvb, dtype=np.float32)
    wo = np.ascontiguousarray(np.asarray(L0_attn_attn_ow, dtype=np.float32))
    ob = np.asarray(L0_attn_attn_ob, dtype=np.float32)
    qw = np.ascontiguousarray(np.asarray(L0_ffn_qweight, dtype=np.int32))
    wsc = np.asarray(L0_ffn_weight_scale, dtype=np.float32).reshape(-1)
    wzp = np.asarray(L0_ffn_weight_zero_point, dtype=np.float32).reshape(-1)
    fb = np.asarray(L0_ffn_bias, dtype=np.float32)

    qkvb_t = np.ascontiguousarray(qkvb.reshape(NOF, 128).T)
    ob_t = np.ascontiguousarray(ob.reshape(NFC, 128).T)
    fb_t = np.ascontiguousarray(fb.reshape(NFC, 128).T)
    wsc_t = np.ascontiguousarray(wsc.reshape(NFC, 128).T)
    wzp_t = np.ascontiguousarray(wzp.reshape(NFC, 128).T)

    if _NC_CACHE[0] is None:
        _NC_CACHE[0] = build()
    nc = _NC_CACHE[0]

    in_maps = []
    for c in range(8):
        b, half = c // 2, c % 2
        xs = np.ascontiguousarray(x[b, half * TOK:(half + 1) * TOK, :])
        in_maps.append({
            "x": xs, "wqkv": wqkv, "wo": wo, "qw": qw,
            "qkvb_t": qkvb_t, "ob_t": ob_t, "fb_t": fb_t,
            "wsc_t": wsc_t, "wzp_t": wzp_t,
        })
    res = run_bass_kernel_spmd(nc, in_maps, core_ids=list(range(8)))
    kernel._LAST = res
    outp = np.empty((4, 4096, DM), dtype=np.float32)
    for c in range(8):
        b, half = c // 2, c % 2
        outp[b, half * TOK:(half + 1) * TOK, :] = res.results[c]["out"]
    return outp


# version 04
# speedup vs baseline: 1.0110x; 1.0110x over previous
import numpy as np
import contextlib
import concourse.bass as bass
import concourse.mybir as mybir
from concourse.bass_utils import run_bass_kernel_spmd

F32 = mybir.dt.float32
BF16 = mybir.dt.bfloat16
I32 = mybir.dt.int32
AF = mybir.ActivationFunctionType
ALU = mybir.AluOpType
AX = mybir.AxisListType

DM = 4096
TOK = 2048
NT = TOK // 128
NS = TOK // 512
NFC = DM // 128
HEADS = 32
HD = 128
NOF = 96
SM_SCALE = 0.08838834764831845
ESHIFT = -16.0
EPS = 1e-5
C_ROUND = 12582912.0
DEBUG = False

ENGS = ("sync", "gpsimd", "tensor", "vector", "scalar")


class Sem:
    __slots__ = ("h", "n")

    def __init__(self, h):
        self.h = h
        self.n = 0


class Prog:
    def __init__(self):
        self.p = {e: [] for e in ENGS}

    def op(self, eng, fn, waits=(), inc=None, inc_amt=1):
        if inc is not None:
            inc.n += inc_amt
        v = inc.n if inc is not None else None
        waits = tuple((s, val) for (s, val) in waits if val and val > 0)

        def run(e, fn=fn, waits=waits, inc=inc, inc_amt=inc_amt):
            for (s, val) in waits:
                e.wait_ge(s.h, val)
            inst = fn(e)
            if inc is not None:
                inst.then_inc(inc.h, inc_amt)

        self.p[eng].append(run)
        return v

    def raw(self, eng, fn):
        self.p[eng].append(fn)


def fm(t, chunk, c0, c1):
    return t[:, chunk * TOK + c0: chunk * TOK + c1]


def build():
    nc = bass.Bass("TRN2", target_bir_lowering=False)

    x_in = nc.dram_tensor("x", [TOK, DM], F32, kind="ExternalInput").ap()
    wqkv = nc.dram_tensor("wqkv", [DM, 3 * DM], F32, kind="ExternalInput").ap()
    wo = nc.dram_tensor("wo", [DM, DM], F32, kind="ExternalInput").ap()
    qwq = nc.dram_tensor("qw", [DM, 3072], I32, kind="ExternalInput").ap()
    qkvb_t = nc.dram_tensor("qkvb_t", [128, NOF], F32, kind="ExternalInput").ap()
    ob_t_in = nc.dram_tensor("ob_t", [128, NFC], F32, kind="ExternalInput").ap()
    fb_t_in = nc.dram_tensor("fb_t", [128, NFC], F32, kind="ExternalInput").ap()
    wsc_in = nc.dram_tensor("wsc_t", [128, NFC], F32, kind="ExternalInput").ap()
    wzp_in = nc.dram_tensor("wzp_t", [128, NFC], F32, kind="ExternalInput").ap()
    out = nc.dram_tensor("out", [TOK, DM], F32, kind="ExternalOutput").ap()

    q_scr = nc.dram_tensor("q_scr", [DM, TOK], BF16, kind="Internal").ap()
    kcc_in_t = nc.dram_tensor("kcc_in", [DM, TOK], BF16, kind="Internal")
    kcc_out_t = nc.dram_tensor("kcc_out", [2 * DM, TOK], BF16, kind="Internal")
    vcc_in_t = nc.dram_tensor("vcc_in", [512, 128, 128], BF16, kind="Internal")
    vcc_out_t = nc.dram_tensor("vcc_out", [1024, 128, 128], BF16, kind="Internal")
    kcc_in, kcc_out = kcc_in_t.ap(), kcc_out_t.ap()
    vcc_in, vcc_out = vcc_in_t.ap(), vcc_out_t.ap()
    attn_scr = nc.dram_tensor("attn_scr", [512, 128, 128], BF16, kind="Internal").ap()
    ffn_scr = nc.dram_tensor("ffn_scr", [512, 128, 128], BF16, kind="Internal").ap()
    x2_scr = nc.dram_tensor("x2_scr", [TOK, DM], F32, kind="Internal").ap()

    dbg = {}
    if DEBUG:
        for nm, shp, dt in [("d_q", [DM, TOK], BF16), ("d_k", [DM, TOK], BF16),
                            ("d_v", [512, 128, 128], BF16),
                            ("d_kg", [2 * DM, TOK], BF16),
                            ("d_ctx", [128, HEADS * TOK], BF16),
                            ("d_attn", [512, 128, 128], BF16),
                            ("d_x2", [TOK, DM], F32),
                            ("d_ffn", [512, 128, 128], BF16),
                            ("d_y1t", [128, 65536], BF16),
                            ("d_wnk0", [128, DM], BF16),
                            ("d_mv0", [128, 8], F32)]:
            dbg[nm] = nc.dram_tensor(nm, shp, dt, kind="ExternalOutput").ap()

    with contextlib.ExitStack() as st:
        def sb(name, shape, dt=F32):
            return st.enter_context(nc.sbuf_tensor(name, list(shape), dt))

        def ps(name, shape, dt=F32):
            return st.enter_context(nc.psum_tensor(name, list(shape), dt))

        big = sb("big", [128, 65536], BF16)
        dbuf = sb("dbuf", [128, 8192], BF16)
        bufC = sb("bufC", [128, DM], F32)
        bufD = sb("bufD", [128, DM], F32)
        stg = sb("stg", [128, DM], BF16)
        wnk = sb("wnk", [128, DM], BF16)
        wstg = sb("wstg", [128, 512], F32)
        pbuf = sb("pbuf", [128, 2048], BF16)
        qbuf = sb("qbuf", [128, 1024], BF16)
        bcast = sb("bcast", [128, 512], F32)
        eye = sb("eye", [128, 128], BF16)
        bqkv = sb("bqkv", [128, NOF], F32)
        bob = sb("bob", [128, NFC], F32)
        bfb = sb("bfb", [128, NFC], F32)
        bsc = sb("bsc", [128, NFC], F32)
        bzp = sb("bzp", [128, NFC], F32)
        stt = sb("stt", [128, 64], F32)
        mv = sb("mv", [128, 8], F32)
        cst = sb("cst", [128, 8], F32)
        ones = sb("ones", [128, 1], BF16)
        onesr = sb("onesr", [1, 128], BF16)
        rrow = sb("rrow", [1, 512], BF16)
        rrowf = sb("rrowf", [1, 512], F32)
        cnf = sb("cnf", [128, 16], BF16)

        pA = ps("pA", [128, 1024])
        pB = ps("pB", [128, 1024])
        pC = ps("pC", [128, 512])
        pD = ps("pD", [128, 512])
        pE = ps("pE", [128, 512])
        pF = ps("pF", [128, 512])

        sL = Sem(st.enter_context(nc.semaphore("sL")))
        sS = Sem(st.enter_context(nc.semaphore("sS")))
        sG = Sem(st.enter_context(nc.semaphore("sG")))
        sCC = Sem(st.enter_context(nc.semaphore("sCC")))
        sPE = Sem(st.enter_context(nc.semaphore("sPE")))
        sVE = Sem(st.enter_context(nc.semaphore("sVE")))
        sAE = Sem(st.enter_context(nc.semaphore("sAE")))
        sPO = Sem(st.enter_context(nc.semaphore("sPO")))

        P = Prog()

        def echo(queue, sem):
            return P.op(queue, lambda e: e.dma_start(out=stt[0:1, 48:64],
                                                     in_=x_in[0:1, 0:16]),
                        inc=sem, inc_amt=16)

        consts = [0.0, 1.0, SM_SCALE, ESHIFT]
        for i, cv in enumerate(consts):
            P.op("gpsimd", (lambda i=i, cv=cv: (lambda e: e.memset(cst[:, i:i + 1], cv)))())
            nc.const_aps.register(F32, cv, cst[:, i:i + 1])
        P.op("gpsimd", lambda e: e.memset(ones[:], 1.0))
        P.op("gpsimd", lambda e: e.memset(onesr[:], 1.0))
        P.op("gpsimd", lambda e: e.memset(eye[:], 0.0))
        setup_done = P.op("gpsimd", lambda e: e.affine_select(
            out=eye[:], in_=eye[:], compare_op=ALU.not_equal, fill=1.0,
            base=0, pattern=[[-1, 128]], channel_multiplier=1), inc=sPO)

        for dst, src in [(bqkv, qkvb_t), (bob, ob_t_in), (bfb, fb_t_in),
                         (bsc, wsc_in), (bzp, wzp_in)]:
            P.op("sync", (lambda dst=dst, src=src: (lambda e: e.dma_start(out=dst[:], in_=src)))(),
                 inc=sL, inc_amt=16)
        bias_ld = echo("sync", sL)

        def emit_ln_stats_apply(xsrc, ydst, ready_waits, apply_extra_waits=()):
            vlast = None
            for i in range(8):
                w = ready_waits if i == 0 else ()
                vlast = P.op("vector", (lambda i=i, xsrc=xsrc: (lambda e: e.bn_stats(
                    stt[:, i * 6:(i + 1) * 6], xsrc[:, i * 512:(i + 1) * 512])))(),
                    waits=w, inc=sVE)
            vlast = P.op("vector", lambda e: e.bn_aggr(mv[:, 0:2], stt[:, 0:48]),
                         waits=[(sVE, vlast)], inc=sVE)
            ve = P.op("vector", lambda e: e.tensor_single_scalar(mv[:, 4:5], mv[:, 1:2], EPS, ALU.add),
                      waits=[(sVE, vlast)], inc=sVE)
            ae = P.op("scalar", lambda e: e.activation(mv[:, 5:6], mv[:, 4:5], AF.Sqrt,
                                                       bias=0.0, scale=1.0),
                      waits=[(sVE, ve), (sPO, setup_done)], inc=sAE)
            vr = P.op("vector", lambda e: e.reciprocal(mv[:, 2:3], mv[:, 5:6]),
                      waits=[(sAE, ae)], inc=sVE)
            ve = P.op("vector", lambda e: e.tensor_scalar(mv[:, 3:4], mv[:, 0:1],
                                                          mv[:, 2:3], -1.0, ALU.mult, ALU.mult),
                      waits=[(sVE, vr)], inc=sVE)
            ae = P.op("scalar", (lambda xsrc=xsrc, ydst=ydst: (lambda e: e.activation(
                ydst[:], xsrc[:], AF.Identity, bias=mv[:, 3:4], scale=mv[:, 2:3])))(),
                waits=[(sVE, ve)] + list(apply_extra_waits), inc=sAE)
            return ae

        tr_state = {"last_evac": {}, "mm": None}

        def emit_transpose_to_big(ysrc, tt, first_waits):
            last_pe = None
            last_ve = None
            for g in range(8):
                bank = pC if g % 2 == 0 else pD
                for j in range(4):
                    fc = g * 4 + j
                    w = []
                    if fc == 0:
                        w += list(first_waits)
                    if j == 0:
                        prev = tr_state["last_evac"].get(g % 2)
                        if prev:
                            w.append((sVE, prev))
                    inc = sPE if j == 3 else None
                    r = P.op("tensor", (lambda bank=bank, j=j, fc=fc, ysrc=ysrc: (lambda e: e.matmul(
                        bank[:, j * 128:(j + 1) * 128], ysrc[:, fc * 128:(fc + 1) * 128],
                        eye[:], start=True, stop=True)))(), waits=w, inc=inc)
                    if inc:
                        last_pe = r
                def mkevac(bank=bank, g=g, tt=tt):
                    def f(e):
                        dst = big[:].rearrange("p (f t) -> p f t", t=TOK)[
                            :, g * 4:(g + 1) * 4, tt * 128:(tt + 1) * 128]
                        src = bank[:].rearrange("p (f j) -> p f j", j=128)
                        return e.tensor_copy(dst, src)
                    return f
                last_ve = P.op("vector", mkevac(), waits=[(sPE, last_pe)], inc=sVE)
                tr_state["last_evac"][g % 2] = last_ve
            last_ve = P.op("scalar", (lambda tt=tt: (lambda e: e.activation(
                big[:].rearrange("p (f t) -> p f t", t=TOK)[:, :, tt * 128:(tt + 1) * 128],
                big[:].rearrange("p (f t) -> p f t", t=TOK)[:, :, tt * 128:(tt + 1) * 128],
                AF.Identity, bias=0.0, scale=1.0)))(),
                waits=[(sVE, last_ve), (sPO, setup_done)], inc=sVE)
            return last_pe, last_ve

        p1_apply = {}
        p1_ve_done = None
        for tt in range(NT):
            xb = bufC if tt % 2 == 0 else bufD
            w = [(sL, bias_ld)] if tt == 0 else []
            if tt >= 2:
                w.append((sAE, p1_apply[tt - 2]))
            xl = P.op("sync", (lambda tt=tt, xb=xb: (lambda e: e.dma_start(
                out=xb[:], in_=x_in[tt * 128:(tt + 1) * 128, :])))(),
                waits=w, inc=sL, inc_amt=16)
            xl = echo("sync", sL)
            aw = [(sPE, tr_state["mm"])] if tr_state["mm"] else []
            ae = emit_ln_stats_apply(xb, wnk, ready_waits=[(sL, xl)], apply_extra_waits=aw)
            p1_apply[tt] = ae
            if DEBUG and tt == 0:
                P.op("scalar", lambda e: e.dma_start(out=dbg["d_wnk0"], in_=wnk[:]),
                     waits=[(sAE, ae)], inc=sS, inc_amt=16)
                P.op("scalar", lambda e: e.dma_start(out=dbg["d_mv0"], in_=mv[:]),
                     waits=[(sAE, ae)], inc=sS, inc_amt=16)
            mmn, p1_ve_done = emit_transpose_to_big(wnk, tt, first_waits=[(sAE, ae)])
            tr_state["mm"] = mmn
        p1_done_ve = p1_ve_done
        if DEBUG:
            P.op("scalar", lambda e: e.dma_start(out=dbg["d_y1t"], in_=big[:]),
                 waits=[(sVE, p1_done_ve)], inc=sS, inc_amt=16)

        BANKS8 = [pA[:, 0:512], pA[:, 512:1024], pB[:, 0:512], pB[:, 512:1024],
                  pC[:, :], pD[:, :], pE[:, :], pF[:, :]]
        BANKS4 = BANKS8[:4]

        wsn = {"cast": {}}
        fcuse = {}
        evac_ae = {}
        stg_store = {}
        gidx_counter = [0]
        wdma = {}

        def weight_stream(wt, col0, gidx, fc, cast_extra=()):
            slot = (gidx * 32 + fc) % 2
            w = []
            prevc = wsn["cast"].get((gidx * 32 + fc) - 2)
            if prevc:
                w.append((sPO, prevc))
            ld = P.op("sync", (lambda wt=wt, col0=col0, fc=fc, slot=slot: (lambda e: e.dma_start(
                out=wstg[:, slot * 256:(slot + 1) * 256],
                in_=wt[fc * 128:(fc + 1) * 128, col0:col0 + 256])))(),
                waits=w, inc=sL, inc_amt=16)
            wdma[(gidx, fc)] = ld
            w2 = [(sL, ld + 16)] + list(cast_extra)
            pu = fcuse.get((gidx - 1, fc))
            if pu:
                w2.append((sPE, pu))
            c = P.op("scalar", (lambda fc=fc, slot=slot: (lambda e: e.activation(
                dbuf[:, fc * 256:(fc + 1) * 256], wstg[:, slot * 256:(slot + 1) * 256],
                AF.Identity, bias=0.0, scale=1.0)))(),
                waits=w2 + [(sPO, setup_done)], inc=sPO)
            wsn["cast"][gidx * 32 + fc] = c

        def qkv_group(of0, kind, extra_first_waits=(), cast_extra=(), last_in_segment=False):
            gidx = gidx_counter[0]
            gidx_counter[0] += 1
            wt = wo if kind == "wo" else wqkv
            col0 = of0 * 128
            for fc in range(NFC):
                weight_stream(wt, col0, gidx, fc, cast_extra=cast_extra)
            if last_in_segment:
                echo("sync", sL)
            act_src = big

            def mm_of_fc(fc, j, sl, bank, start, stop, extra_w=(), inc=None):
                return P.op("tensor", (lambda fc=fc, j=j, sl=sl, bank=bank, start=start, stop=stop:
                            (lambda e: e.matmul(bank, dbuf[:, fc * 256 + j * 128: fc * 256 + (j + 1) * 128],
                                                fm(act_src, fc, sl * 512, (sl + 1) * 512),
                                                start=start, stop=stop)))(),
                            waits=list(extra_w), inc=inc)

            if kind in ("k", "q"):
                passes = [(0, 1, 2, 3)]
                banksel = lambda j, si: BANKS8[j * 4 + si]
            else:
                passes = [(0, 1), (2, 3)]
                banksel = lambda j, si: BANKS4[j * 2 + si]

            for pi, sls in enumerate(passes):
                for fc in range(NFC):
                    first_extra = []
                    if pi == 0:
                        first_extra.append((sPO, wsn["cast"][gidx * 32 + fc]))
                    if fc == 0:
                        if pi > 0:
                            pe_prev = evac_ae.get((gidx, pi - 1))
                        else:
                            pe_prev = evac_ae.get((gidx - 1, "last"))
                        if pe_prev:
                            first_extra.append((sAE, pe_prev))
                        if gidx == 0:
                            first_extra.append((sVE, p1_done_ve))
                        first_extra.extend(extra_first_waits)
                        extra_first_waits = ()
                    nmm = len(sls) * 2
                    cnt = 0
                    for j in range(2):
                        for si, sl in enumerate(sls):
                            cnt += 1
                            inc = sPE if cnt == nmm else None
                            r = mm_of_fc(fc, j, sl, banksel(j, si),
                                         start=(fc == 0), stop=(fc == NFC - 1),
                                         extra_w=first_extra if cnt == 1 else (), inc=inc)
                    fcuse[(gidx, fc)] = r
                last_pe = fcuse[(gidx, NFC - 1)]
                for j in range(2):
                    for si, sl in enumerate(sls):
                        of = of0 + j
                        slotid = (j * len(sls) + si) if kind in ("v", "wo") else (j * 4 + si)
                        w = [(sPE, last_pe)]
                        pv = stg_store.get(slotid)
                        if pv:
                            w.append((sS, pv))
                        bias = bqkv[:, of:of + 1] if kind != "wo" else bob[:, of:of + 1]
                        ae = P.op("scalar", (lambda bank=banksel(j, si), slotid=slotid, bias=bias:
                                  (lambda e: e.activation(stg[:, slotid * 512:(slotid + 1) * 512],
                                                          bank, AF.Identity, bias=bias, scale=1.0)))(),
                                  waits=w, inc=sAE)
                        if kind in ("k", "q"):
                            base = q_scr if kind == "q" else kcc_in
                            r0 = (of - (32 if kind == "k" else 0)) * 128
                            sc = P.op("scalar", (lambda base=base, r0=r0, sl=sl, slotid=slotid:
                                      (lambda e: e.dma_start(
                                          out=base[r0:r0 + 128, sl * 512:(sl + 1) * 512],
                                          in_=stg[:, slotid * 512:(slotid + 1) * 512])))(),
                                      waits=[(sAE, ae)], inc=sS, inc_amt=16)
                            stg_store[slotid] = sc
                        else:
                            tb = pE if (j + si) % 2 == 0 else pF
                            tslot = 4 + (j * len(sls) + si) % 4
                            tpe = None
                            for i4 in range(4):
                                wv = [(sAE, ae)] if i4 == 0 else []
                                pvv = tr_state["last_evac"].get(("p2t", (j + si) % 2))
                                if i4 == 0 and pvv:
                                    wv.append((sVE, pvv))
                                tpe = P.op("tensor", (lambda tb=tb, i4=i4, slotid=slotid:
                                           (lambda e: e.matmul(tb[:, i4 * 128:(i4 + 1) * 128],
                                                               stg[:, slotid * 512 + i4 * 128: slotid * 512 + (i4 + 1) * 128],
                                                               eye[:], start=True, stop=True)))(),
                                           waits=wv, inc=(sPE if i4 == 3 else None))
                            wv2 = [(sPE, tpe)]
                            pv2 = stg_store.get(tslot)
                            if pv2:
                                wv2.append((sS, pv2))
                            ve = P.op("vector", (lambda tb=tb, tslot=tslot: (lambda e: e.tensor_copy(
                                stg[:, tslot * 512:(tslot + 1) * 512], tb[:])))(),
                                waits=wv2, inc=sVE)
                            tr_state["last_evac"][("p2t", (j + si) % 2)] = ve
                            for i4 in range(4):
                                tt = sl * 4 + i4
                                if kind == "v":
                                    blk = vcc_in[(of - 64) * 16 + tt]
                                else:
                                    blk = attn_scr[tt * 32 + of]
                                sc = P.op("scalar", (lambda blk=blk, tslot=tslot, i4=i4:
                                          (lambda e: e.dma_start(
                                              out=blk,
                                              in_=stg[:, tslot * 512 + i4 * 128: tslot * 512 + (i4 + 1) * 128])))(),
                                          waits=[(sVE, ve)], inc=sS, inc_amt=16)
                            stg_store[tslot] = sc
                evac_ae[(gidx, pi)] = sAE.n
            evac_ae[(gidx, "last")] = sAE.n
            return gidx

        for gof in range(16):
            qkv_group(64 + gof * 2, "v")
        v_stores_done = sS.n
        cc_v = None
        for gof in range(16):
            qkv_group(32 + gof * 2, "k")
        k_stores_done = sS.n
        cc_v = P.op("gpsimd", lambda e: e.collective_compute(
            "AllGather", ALU.bypass,
            replica_groups=[[0, 1], [2, 3], [4, 5], [6, 7]],
            ins=[vcc_in_t.ap().opt()], outs=[vcc_out_t.ap().opt()]),
            waits=[(sS, v_stores_done)], inc=sCC)
        cc_k = None
        for gof in range(16):
            qkv_group(gof * 2, "q", last_in_segment=(gof == 15))
            if gof == 0:
                cc_k = P.op("gpsimd", lambda e: e.collective_compute(
                    "AllGather", ALU.bypass,
                    replica_groups=[[0, 1], [2, 3], [4, 5], [6, 7]],
                    ins=[kcc_in_t.ap().opt()], outs=[kcc_out_t.ap().opt()]),
                    waits=[(sS, k_stores_done)], inc=sCC)
        q_stores_done = sS.n
        p2_last_pe = sPE.n
        p2_last_ae = sAE.n

        if DEBUG:
            P.op("sync", lambda e: e.dma_start(out=dbg["d_q"], in_=q_scr),
                 waits=[(sS, q_stores_done)], inc=sL, inc_amt=16)
            P.op("sync", lambda e: e.dma_start(out=dbg["d_k"], in_=kcc_in),
                 waits=[(sS, k_stores_done)], inc=sL, inc_amt=16)
            P.op("sync", lambda e: e.dma_start(out=dbg["d_v"], in_=vcc_in),
                 waits=[(sS, v_stores_done)], inc=sL, inc_amt=16)
            P.op("sync", lambda e: e.dma_start(out=dbg["d_kg"], in_=kcc_out),
                 waits=[(sCC, cc_k)], inc=sL, inc_amt=16)

        kv_ld = {}
        s_done = {}
        exp_done = {}
        ctx_done = {}
        smm_last = {}
        ctx_evac = {}
        head_last_mm = {}
        bc_copy = {}
        qld = {}

        for h in range(HEADS):
            w = [(sCC, cc_k), (sS, q_stores_done)]
            if h == 0:
                w.append((sPE, p2_last_pe))
            else:
                w.append((sPE, head_last_mm[h - 1]))
            for r in range(2):
                P.op("sync", (lambda h=h, r=r: (lambda e: e.dma_start(
                    out=dbuf[:, r * 2048:(r + 1) * 2048],
                    in_=kcc_out[r * DM + h * 128: r * DM + (h + 1) * 128, :])))(),
                    waits=w if r == 0 else [], inc=sL, inc_amt=16)
            for r in range(2):
                P.op("sync", (lambda h=h, r=r: (lambda e: e.dma_start(
                    out=dbuf[:, 4096 + r * 2048: 4096 + (r + 1) * 2048].rearrange(
                        "p (t j) -> p t j", j=128),
                    in_=vcc_out[r * 512 + h * 16: r * 512 + (h + 1) * 16].rearrange(
                        "t p j -> p t j"))))(),
                    waits=[(sCC, cc_v)] if r == 0 else [], inc=sL, inc_amt=16)
            kv_ld[h] = echo("sync", sL)

            for qt in range(4):
                i = h * 4 + qt
                wq = []
                if i >= 2 and smm_last.get(i - 2):
                    wq.append((sPE, smm_last[i - 2]))
                P.op("sync", (lambda h=h, qt=qt, i=i: (lambda e: e.dma_start(
                    out=qbuf[:, (i % 2) * 512:(i % 2) * 512 + 512],
                    in_=q_scr[h * 128:(h + 1) * 128, qt * 512:(qt + 1) * 512])))(),
                    waits=wq, inc=sL, inc_amt=16)
                qld[i] = echo("sync", sL)

                ctxbank = pC if i % 2 == 0 else pD
                lbank = pE if i % 2 == 0 else pF
                for jp in range(16):
                    par = jp % 2
                    sb_ = pA if par == 0 else pB
                    gpair = i * 16 + jp
                    w1 = []
                    pe_prev = exp_done.get(gpair - 2)
                    if pe_prev:
                        w1.append((sAE, pe_prev))
                    if jp == 0:
                        w1 += [(sL, qld[i]), (sL, kv_ld[h])]
                        pv = ctx_evac.get(i - 2)
                        if pv:
                            w1.append((sVE, pv))
                        pv = bc_copy.get(i - 1)
                        if pv:
                            w1.append((sVE, pv))
                        if i == 0:
                            w1.append((sPO, setup_done))
                    for half in range(2):
                        kc = jp * 2 + half
                        rsel, idx = kc // 16, kc % 16
                        r = P.op("tensor", (lambda sb_=sb_, half=half, rsel=rsel, idx=idx, i=i:
                                 (lambda e: e.matmul(
                                     sb_[:, half * 512:(half + 1) * 512],
                                     dbuf[:, rsel * 2048 + idx * 128: rsel * 2048 + (idx + 1) * 128],
                                     qbuf[:, (i % 2) * 512:(i % 2) * 512 + 512],
                                     start=True, stop=True)))(),
                                 waits=w1 if half == 0 else (), inc=(sPE if half == 1 else None))
                    s_done[gpair] = r
                    smm_last[i] = r
                    w2 = [(sPE, s_done[gpair])]
                    pv = ctx_done.get(gpair - 2)
                    if pv:
                        w2.append((sPE, pv))
                    exp_done[gpair] = P.op("scalar", (lambda sb_=sb_, par=par: (lambda e: e.activation(
                        pbuf[:, par * 1024:(par + 1) * 1024], sb_[:],
                        AF.Exp, bias=ESHIFT, scale=SM_SCALE)))(),
                        waits=w2, inc=sAE)
                    w3 = [(sAE, exp_done[gpair])]
                    lastmm = None
                    for half in range(2):
                        kc = jp * 2 + half
                        rsel, idx = kc // 16, kc % 16
                        P.op("tensor", (lambda ctxbank=ctxbank, par=par, half=half, rsel=rsel, idx=idx, kc=kc:
                             (lambda e: e.matmul(
                                 ctxbank[:], dbuf[:, 4096 + rsel * 2048 + idx * 128: 4096 + rsel * 2048 + (idx + 1) * 128],
                                 pbuf[:, par * 1024 + half * 512: par * 1024 + (half + 1) * 512],
                                 start=(kc == 0), stop=(kc == 31))))(),
                             waits=w3 if half == 0 else ())
                    for half in range(2):
                        kc = jp * 2 + half
                        lastmm = P.op("tensor", (lambda lbank=lbank, par=par, half=half, kc=kc:
                                      (lambda e: e.matmul(
                                          lbank[0:1, :], ones[:],
                                          pbuf[:, par * 1024 + half * 512: par * 1024 + (half + 1) * 512],
                                          start=(kc == 0), stop=(kc == 31))))(),
                                      inc=(sPE if half == 1 else None))
                    ctx_done[gpair] = lastmm
                vr = P.op("vector", (lambda lbank=lbank: (lambda e: e.reciprocal(
                    rrowf[:], lbank[0:1, :])))(), waits=[(sPE, ctx_done[i * 16 + 15])], inc=sVE)
                ve1 = P.op("scalar", lambda e: e.activation(rrow[:], rrowf[:],
                           AF.Identity, bias=0.0, scale=1.0),
                           waits=[(sVE, vr)], inc=sVE)
                wb = [(sVE, ve1), (sAE, exp_done[i * 16 + 14])]
                bpe = P.op("tensor", lambda e: e.matmul(
                    pA[:, 0:512], onesr[:], rrow[:], start=True, stop=True),
                    waits=wb, inc=sPE)
                ve2 = P.op("vector", lambda e: e.tensor_copy(bcast[:], pA[:, 0:512]),
                           waits=[(sPE, bpe)], inc=sVE)
                bc_copy[i] = ve2
                cx = P.op("vector", (lambda ctxbank=ctxbank, h=h, qt=qt: (lambda e: e.tensor_mul(
                    big[:, h * TOK + qt * 512: h * TOK + (qt + 1) * 512],
                    ctxbank[:], bcast[:])))(), waits=[(sVE, ve2)], inc=sVE)
                ctx_evac[i] = P.op("vector", (lambda h=h, qt=qt: (lambda e: e.tensor_copy(
                    cnf[:], big[:, h * TOK + qt * 512 + 496: h * TOK + qt * 512 + 512])))(),
                    waits=[(sVE, cx)], inc=sVE)
                head_last_mm[h] = ctx_done[i * 16 + 15]
        p3_done_ve = sVE.n
        p3_done_pe = sPE.n

        if DEBUG:
            P.op("scalar", lambda e: e.dma_start(out=dbg["d_ctx"], in_=big[:]),
                 waits=[(sVE, p3_done_ve)], inc=sS, inc_amt=16)

        p3_seal = P.op("scalar", lambda e: e.activation(
            big[:, 31 * TOK + 3 * 512: 31 * TOK + 4 * 512],
            big[:, 31 * TOK + 3 * 512: 31 * TOK + 4 * 512],
            AF.Identity, bias=0.0, scale=1.0), waits=[(sVE, sVE.n)], inc=sAE)
        p3_exp_last = sAE.n
        for gof in range(16):
            qkv_group(gof * 2, "wo",
                      extra_first_waits=((sVE, p3_done_ve), (sAE, p3_exp_last)) if gof == 0 else (),
                      cast_extra=((sPE, p3_done_pe),) if gof == 0 else (),
                      last_in_segment=(gof == 15))
        p4_stores_done = sS.n
        p4_last_pe = sPE.n

        if DEBUG:
            P.op("sync", lambda e: e.dma_start(out=dbg["d_attn"], in_=attn_scr),
                 waits=[(sS, p4_stores_done)], inc=sL, inc_amt=16)

        x2_store = {}
        ln2_ae = {}
        ln3_ae = {}
        add_ve = {}
        for tt in range(NT):
            wa = [(sS, p4_stores_done)] if tt == 0 else [(sAE, ln2_ae[tt - 1])]
            al = P.op("sync", (lambda tt=tt: (lambda e: e.dma_start(
                out=stg[:].rearrange("p (t j) -> p t j", j=128),
                in_=attn_scr[tt * 32:(tt + 1) * 32].rearrange("t p j -> p t j"))))(),
                waits=wa, inc=sL, inc_amt=16)
            al = echo("sync", sL)
            wx = []
            if tt >= 1:
                wx.append((sAE, ln3_ae[tt - 1]))
                wx.append((sS, x2_store[tt - 1]))
            xl = P.op("sync", (lambda tt=tt: (lambda e: e.dma_start(
                out=bufC[:], in_=x_in[tt * 128:(tt + 1) * 128, :])))(),
                waits=wx, inc=sL, inc_amt=16)
            xl = echo("sync", sL)
            wd = [(sVE, add_ve[tt - 1])] if tt >= 1 else []
            ln2_ae[tt] = emit_ln_stats_apply(stg, bufD, ready_waits=[(sL, al)],
                                             apply_extra_waits=wd)
            add_ve[tt] = P.op("vector", lambda e: e.tensor_add(bufC[:], bufC[:], bufD[:]),
                              waits=[(sAE, ln2_ae[tt]), (sL, xl)], inc=sVE)
            x2_store[tt] = P.op("scalar", (lambda tt=tt: (lambda e: e.dma_start(
                out=x2_scr[tt * 128:(tt + 1) * 128, :], in_=bufC[:])))(),
                waits=[(sVE, add_ve[tt])], inc=sS, inc_amt=16)
            aw = [(sPE, tr_state["mm"])] if tr_state["mm"] else []
            ln3_ae[tt] = emit_ln_stats_apply(bufC, wnk, ready_waits=[(sVE, add_ve[tt])],
                                             apply_extra_waits=aw)
            first = [(sAE, ln3_ae[tt])]
            if tt == 0:
                first.append((sPE, p4_last_pe))
            mmn, _ = emit_transpose_to_big(wnk, tt, first_waits=first)
            tr_state["mm"] = mmn
        p5_done_ve = sVE.n
        p5_last_ae = sAE.n
        x2_all = sS.n

        if DEBUG:
            P.op("scalar", lambda e: e.dma_start(out=dbg["d_x2"], in_=x2_scr),
                 waits=[(sS, x2_all)], inc=sS, inc_amt=16)

        qw_ld = {}
        deq_done = {}
        wT_evac = {}
        ffn_mm_done = {}
        ffn_stores = {}
        for n in range(NFC):
            w = []
            if n == 0:
                w = [(sS, x2_all), (sAE, p5_last_ae), (sVE, p5_done_ve)]
            else:
                w = [(sVE, deq_done[n - 1])]
            P.op("gpsimd", (lambda n=n: (lambda e: e.dma_start(
                out=bufC[:, 0:3072], in_=qwq[n * 128:(n + 1) * 128, :])))(),
                waits=w, inc=sG, inc_amt=16)
            qw_ld[n] = echo("gpsimd", sG)
            wv = [(sG, qw_ld[n])]
            if tr_state["mm"]:
                wv.append((sPE, tr_state["mm"]))
            P.op("vector", lambda e: e.tensor_scalar(
                bufD[:, 0:3072], bufC[:, 0:3072], 1.0 / 64.0, -0.4921875,
                ALU.mult, ALU.add), waits=wv)
            P.op("vector", lambda e: e.tensor_scalar(
                bufD[:, 0:3072], bufD[:, 0:3072], C_ROUND, -C_ROUND,
                ALU.add, ALU.add))
            P.op("vector", lambda e: e.scalar_tensor_tensor(
                bufC[:, 0:3072], bufD[:, 0:3072], -64.0, bufC[:, 0:3072],
                ALU.mult, ALU.add))

            vm = P.op("vector", (lambda n=n: (lambda e: e.tensor_scalar(
                mv[:, 6:7], bzp[:, n:n + 1], bsc[:, n:n + 1], -1.0,
                ALU.mult, ALU.mult)))(), inc=sVE)

            def wv_(src, j, outj, n=n):
                sv = src[:, 0:3072].rearrange("p (g w) -> p g w", w=3)[:, :, j]
                ov = wnk[:, 0:DM].rearrange("p (g f) -> p g f", f=4)[:, :, outj]
                def f(e):
                    return e.activation(ov, sv, AF.Identity,
                                        bias=mv[:, 6:7], scale=bsc[:, n:n + 1])
                return f
            P.op("scalar", wv_(bufC, 0, 0), waits=[(sVE, vm)])
            P.op("scalar", wv_(bufD, 0, 1))
            P.op("scalar", wv_(bufC, 1, 2))
            deq_done[n] = P.op("scalar", wv_(bufC, 2, 3), inc=sVE)

            tslot = n % 2
            last_pe = None
            for g in range(8):
                bank = pC if g % 2 == 0 else pD
                for j in range(4):
                    fc = g * 4 + j
                    w = []
                    if fc == 0:
                        w.append((sVE, deq_done[n]))
                        if n >= 2 and ffn_mm_done.get(n - 2):
                            w.append((sPE, ffn_mm_done[n - 2]))
                        if n == 0:
                            w.append((sPE, p4_last_pe))
                    if j == 0:
                        pv = tr_state["last_evac"].get(("p6", g % 2))
                        if pv:
                            w.append((sVE, pv))
                    last_pe = P.op("tensor", (lambda bank=bank, j=j, fc=fc: (lambda e: e.matmul(
                        bank[:, j * 128:(j + 1) * 128], wnk[:, fc * 128:(fc + 1) * 128],
                        eye[:], start=True, stop=True)))(),
                        waits=w, inc=(sPE if j == 3 else None))
                ve = P.op("scalar", (lambda bank=bank, g=g, tslot=tslot: (lambda e: e.activation(
                    dbuf[:, tslot * 4096 + g * 512:tslot * 4096 + (g + 1) * 512], bank[:],
                    AF.Identity, bias=0.0, scale=1.0)))(),
                    waits=[(sPE, last_pe)], inc=sVE)
                tr_state["last_evac"][("p6", g % 2)] = ve
            wT_evac[n] = ve
            tr_state["mm"] = last_pe

            last = None
            for kc in range(NFC):
                for sl in range(4):
                    w = []
                    if kc == 0 and sl == 0:
                        w.append((sVE, wT_evac[n]))
                        pv = ffn_stores.get(n - 1)
                        if pv:
                            w.append((sAE, ffn_stores[n - 1][0]))
                    last = P.op("tensor", (lambda kc=kc, sl=sl, tslot=tslot: (lambda e: e.matmul(
                        BANKS4[sl], dbuf[:, tslot * 4096 + kc * 128: tslot * 4096 + (kc + 1) * 128],
                        fm(big, kc, sl * 512, (sl + 1) * 512),
                        start=(kc == 0), stop=(kc == NFC - 1))))(),
                        waits=w, inc=(sPE if (kc == NFC - 1 and sl == 3) else None))
            ffn_mm_done[n] = last

            ae_last = None
            for sl in range(4):
                w = [(sPE, ffn_mm_done[n])]
                pv = stg_store.get(("p6", sl))
                if pv:
                    w.append((sS, pv))
                ae = P.op("scalar", (lambda sl=sl, n=n: (lambda e: e.activation(
                    stg[:, sl * 512:(sl + 1) * 512], BANKS4[sl], AF.Identity,
                    bias=bfb[:, n:n + 1], scale=1.0)))(), waits=w, inc=sAE)
                tb = pE if sl % 2 == 0 else pF
                tpe = None
                for i4 in range(4):
                    w2 = [(sAE, ae)] if i4 == 0 else []
                    pv2 = tr_state["last_evac"].get(("p6o", sl % 2))
                    if i4 == 0 and pv2:
                        w2.append((sVE, pv2))
                    tpe = P.op("tensor", (lambda tb=tb, i4=i4, sl=sl: (lambda e: e.matmul(
                        tb[:, i4 * 128:(i4 + 1) * 128],
                        stg[:, sl * 512 + i4 * 128: sl * 512 + (i4 + 1) * 128],
                        eye[:], start=True, stop=True)))(),
                        waits=w2, inc=(sPE if i4 == 3 else None))
                w3 = [(sPE, tpe)]
                pv3 = stg_store.get(("p6b", sl))
                if pv3:
                    w3.append((sS, pv3))
                ve = P.op("vector", (lambda tb=tb, sl=sl: (lambda e: e.tensor_copy(
                    stg[:, (4 + sl) * 512:(5 + sl) * 512], tb[:])))(),
                    waits=w3, inc=sVE)
                tr_state["last_evac"][("p6o", sl % 2)] = ve
                sc = None
                for i4 in range(4):
                    tt = sl * 4 + i4
                    sc = P.op("scalar", (lambda tt=tt, n=n, sl=sl, i4=i4: (lambda e: e.dma_start(
                        out=ffn_scr[tt * 32 + n],
                        in_=stg[:, (4 + sl) * 512 + i4 * 128: (4 + sl) * 512 + (i4 + 1) * 128])))(),
                        waits=[(sVE, ve)], inc=sS, inc_amt=16)
                stg_store[("p6", sl)] = sS.n
                stg_store[("p6b", sl)] = sc
                ae_last = ae
            ffn_stores[n] = (ae_last, sS.n)
        p6_stores_done = sS.n

        if DEBUG:
            P.op("sync", lambda e: e.dma_start(out=dbg["d_ffn"], in_=ffn_scr),
                 waits=[(sS, p6_stores_done)], inc=sL, inc_amt=16)

        ln4_ae = {}
        out_store = {}
        add7 = {}
        for tt in range(NT):
            wa = [(sS, p6_stores_done)] if tt == 0 else [(sAE, ln4_ae[tt - 1])]
            fl = P.op("sync", (lambda tt=tt: (lambda e: e.dma_start(
                out=stg[:].rearrange("p (t j) -> p t j", j=128),
                in_=ffn_scr[tt * 32:(tt + 1) * 32].rearrange("t p j -> p t j"))))(),
                waits=wa, inc=sL, inc_amt=16)
            fl = echo("sync", sL)
            wx = []
            if tt >= 1:
                wx.append((sVE, add7[tt - 1]))
            xl = P.op("sync", (lambda tt=tt: (lambda e: e.dma_start(
                out=bufC[:], in_=x2_scr[tt * 128:(tt + 1) * 128, :])))(),
                waits=wx, inc=sL, inc_amt=16)
            xl = echo("sync", sL)
            wd = [(sS, out_store[tt - 1])] if tt >= 1 else []
            ln4_ae[tt] = emit_ln_stats_apply(stg, bufD, ready_waits=[(sL, fl)],
                                             apply_extra_waits=wd)
            add7[tt] = P.op("vector", lambda e: e.tensor_add(bufD[:], bufC[:], bufD[:]),
                            waits=[(sAE, ln4_ae[tt]), (sL, xl)], inc=sVE)
            out_store[tt] = P.op("scalar", (lambda tt=tt: (lambda e: e.dma_start(
                out=out[tt * 128:(tt + 1) * 128, :], in_=bufD[:])))(),
                waits=[(sVE, add7[tt])], inc=sS, inc_amt=16)

        final_sS = sS.n
        P.raw("sync", lambda e: e.wait_ge(sS.h, final_sS))
        P.raw("gpsimd", lambda e: e.wait_ge(sS.h, final_sS))

        with nc.Block() as block:
            @block.sync
            def _(e):
                for f in P.p["sync"]:
                    f(e)

            @block.gpsimd
            def _(e):
                for f in P.p["gpsimd"]:
                    f(e)

            @block.tensor
            def _(e):
                for f in P.p["tensor"]:
                    f(e)

            @block.vector
            def _(e):
                for f in P.p["vector"]:
                    f(e)

            @block.scalar
            def _(e):
                for f in P.p["scalar"]:
                    f(e)
    return nc


_NC_CACHE = [None]


def kernel(L0_attn_x, L0_attn_attn_qkvw, L0_attn_attn_qkvb, L0_attn_attn_ow,
           L0_attn_attn_ob, L0_ffn_qweight, L0_ffn_weight_scale,
           L0_ffn_weight_zero_point, L0_ffn_bias):
    kernel._LAST = None
    x = np.ascontiguousarray(np.asarray(L0_attn_x, dtype=np.float32))
    wqkv = np.ascontiguousarray(np.asarray(L0_attn_attn_qkvw, dtype=np.float32))
    qkvb = np.asarray(L0_attn_attn_qkvb, dtype=np.float32)
    wo = np.ascontiguousarray(np.asarray(L0_attn_attn_ow, dtype=np.float32))
    ob = np.asarray(L0_attn_attn_ob, dtype=np.float32)
    qw = np.ascontiguousarray(np.asarray(L0_ffn_qweight, dtype=np.int32))
    wsc = np.asarray(L0_ffn_weight_scale, dtype=np.float32).reshape(-1)
    wzp = np.asarray(L0_ffn_weight_zero_point, dtype=np.float32).reshape(-1)
    fb = np.asarray(L0_ffn_bias, dtype=np.float32)

    qkvb_t = np.ascontiguousarray(qkvb.reshape(NOF, 128).T)
    ob_t = np.ascontiguousarray(ob.reshape(NFC, 128).T)
    fb_t = np.ascontiguousarray(fb.reshape(NFC, 128).T)
    wsc_t = np.ascontiguousarray(wsc.reshape(NFC, 128).T)
    wzp_t = np.ascontiguousarray(wzp.reshape(NFC, 128).T)

    if _NC_CACHE[0] is None:
        _NC_CACHE[0] = build()
    nc = _NC_CACHE[0]

    in_maps = []
    for c in range(8):
        b, half = c // 2, c % 2
        xs = np.ascontiguousarray(x[b, half * TOK:(half + 1) * TOK, :])
        in_maps.append({
            "x": xs, "wqkv": wqkv, "wo": wo, "qw": qw,
            "qkvb_t": qkvb_t, "ob_t": ob_t, "fb_t": fb_t,
            "wsc_t": wsc_t, "wzp_t": wzp_t,
        })
    res = run_bass_kernel_spmd(nc, in_maps, core_ids=list(range(8)))
    kernel._LAST = res
    outp = np.empty((4, 4096, DM), dtype=np.float32)
    for c in range(8):
        b, half = c // 2, c % 2
        outp[b, half * TOK:(half + 1) * TOK, :] = res.results[c]["out"]
    return outp


# version 06
# speedup vs baseline: 1.1680x; 1.1553x over previous
import numpy as np
import contextlib
import concourse.bass as bass
import concourse.mybir as mybir
from concourse.bass_utils import run_bass_kernel_spmd

F32 = mybir.dt.float32
BF16 = mybir.dt.bfloat16
I32 = mybir.dt.int32
AF = mybir.ActivationFunctionType
ALU = mybir.AluOpType
AX = mybir.AxisListType

DM = 4096
TOK = 2048
NT = TOK // 128
NS = TOK // 512
NFC = DM // 128
HEADS = 32
HD = 128
NOF = 96
SM_SCALE = 0.08838834764831845
ESHIFT = -16.0
EPS = 1e-5
C_ROUND = 12582912.0
DEBUG = False

ENGS = ("sync", "gpsimd", "tensor", "vector", "scalar")


class Sem:
    __slots__ = ("h", "n")

    def __init__(self, h):
        self.h = h
        self.n = 0


class Prog:
    def __init__(self):
        self.p = {e: [] for e in ENGS}

    def op(self, eng, fn, waits=(), inc=None, inc_amt=1):
        if inc is not None:
            inc.n += inc_amt
        v = inc.n if inc is not None else None
        waits = tuple((s, val) for (s, val) in waits if val and val > 0)

        def run(e, fn=fn, waits=waits, inc=inc, inc_amt=inc_amt):
            for (s, val) in waits:
                e.wait_ge(s.h, val)
            inst = fn(e)
            if inc is not None:
                inst.then_inc(inc.h, inc_amt)

        self.p[eng].append(run)
        return v

    def raw(self, eng, fn):
        self.p[eng].append(fn)


def fm(t, chunk, c0, c1):
    return t[:, chunk * TOK + c0: chunk * TOK + c1]


def build():
    nc = bass.Bass("TRN2", target_bir_lowering=False)

    x_in = nc.dram_tensor("x", [TOK, DM], F32, kind="ExternalInput").ap()
    wqkv = nc.dram_tensor("wqkv", [DM, 3 * DM], F32, kind="ExternalInput").ap()
    wo = nc.dram_tensor("wo", [DM, DM], F32, kind="ExternalInput").ap()
    qwq = nc.dram_tensor("qw", [DM, 3072], I32, kind="ExternalInput").ap()
    qkvb_t = nc.dram_tensor("qkvb_t", [128, NOF], F32, kind="ExternalInput").ap()
    ob_t_in = nc.dram_tensor("ob_t", [128, NFC], F32, kind="ExternalInput").ap()
    fb_t_in = nc.dram_tensor("fb_t", [128, NFC], F32, kind="ExternalInput").ap()
    wsc_in = nc.dram_tensor("wsc_t", [128, NFC], F32, kind="ExternalInput").ap()
    wzp_in = nc.dram_tensor("wzp_t", [128, NFC], F32, kind="ExternalInput").ap()
    out = nc.dram_tensor("out", [TOK, DM], F32, kind="ExternalOutput").ap()

    q_scr = nc.dram_tensor("q_scr", [DM, TOK], BF16, kind="Internal").ap()
    kcc_in_t = nc.dram_tensor("kcc_in", [DM, TOK], BF16, kind="Internal")
    kcc_out_t = nc.dram_tensor("kcc_out", [2 * DM, TOK], BF16, kind="Internal")
    vcc_in_t = nc.dram_tensor("vcc_in", [512, 128, 128], BF16, kind="Internal")
    vcc_out_t = nc.dram_tensor("vcc_out", [1024, 128, 128], BF16, kind="Internal")
    kcc_in, kcc_out = kcc_in_t.ap(), kcc_out_t.ap()
    vcc_in, vcc_out = vcc_in_t.ap(), vcc_out_t.ap()
    attn_scr = nc.dram_tensor("attn_scr", [512, 128, 128], BF16, kind="Internal").ap()
    ffn_scr = nc.dram_tensor("ffn_scr", [512, 128, 128], BF16, kind="Internal").ap()
    x2_scr = nc.dram_tensor("x2_scr", [TOK, DM], F32, kind="Internal").ap()

    dbg = {}
    if DEBUG:
        for nm, shp, dt in [("d_q", [DM, TOK], BF16), ("d_k", [DM, TOK], BF16),
                            ("d_v", [512, 128, 128], BF16),
                            ("d_kg", [2 * DM, TOK], BF16),
                            ("d_ctx", [128, HEADS * TOK], BF16),
                            ("d_attn", [512, 128, 128], BF16),
                            ("d_x2", [TOK, DM], F32),
                            ("d_ffn", [512, 128, 128], BF16),
                            ("d_y1t", [128, 65536], BF16),
                            ("d_wnk0", [128, DM], BF16),
                            ("d_mv0", [128, 8], F32)]:
            dbg[nm] = nc.dram_tensor(nm, shp, dt, kind="ExternalOutput").ap()

    with contextlib.ExitStack() as st:
        def sb(name, shape, dt=F32):
            return st.enter_context(nc.sbuf_tensor(name, list(shape), dt))

        def ps(name, shape, dt=F32):
            return st.enter_context(nc.psum_tensor(name, list(shape), dt))

        big = sb("big", [128, 65536], BF16)
        dbuf = sb("dbuf", [128, 8192], BF16)
        bufC = sb("bufC", [128, DM], F32)
        bufD = sb("bufD", [128, DM], F32)
        stg = sb("stg", [128, DM], BF16)
        wnk = sb("wnk", [128, DM], BF16)
        wstg = sb("wstg", [128, 768], F32)
        pbuf = sb("pbuf", [128, 2048], BF16)
        qbuf = sb("qbuf", [128, 1024], BF16)
        bcast = sb("bcast", [128, 512], F32)
        eye = sb("eye", [128, 128], BF16)
        bqkv = sb("bqkv", [128, NOF], F32)
        bob = sb("bob", [128, NFC], F32)
        bfb = sb("bfb", [128, NFC], F32)
        bsc = sb("bsc", [128, NFC], F32)
        bzp = sb("bzp", [128, NFC], F32)
        stt = sb("stt", [128, 64], F32)
        mv = sb("mv", [128, 8], F32)
        cst = sb("cst", [128, 8], F32)
        ones = sb("ones", [128, 1], BF16)
        onesr = sb("onesr", [1, 128], BF16)
        rrow = sb("rrow", [1, 512], BF16)
        rrowf = sb("rrowf", [1, 512], F32)
        cnf = sb("cnf", [128, 16], BF16)

        pA = ps("pA", [128, 1024])
        pB = ps("pB", [128, 1024])
        pC = ps("pC", [128, 512])
        pD = ps("pD", [128, 512])
        pE = ps("pE", [128, 512])
        pF = ps("pF", [128, 512])

        sL = Sem(st.enter_context(nc.semaphore("sL")))
        sS = Sem(st.enter_context(nc.semaphore("sS")))
        sG = Sem(st.enter_context(nc.semaphore("sG")))
        sCC = Sem(st.enter_context(nc.semaphore("sCC")))
        sPE = Sem(st.enter_context(nc.semaphore("sPE")))
        sVE = Sem(st.enter_context(nc.semaphore("sVE")))
        sAE = Sem(st.enter_context(nc.semaphore("sAE")))
        sPO = Sem(st.enter_context(nc.semaphore("sPO")))

        P = Prog()

        def echo(queue, sem):
            return P.op(queue, lambda e: e.dma_start(out=stt[0:1, 48:64],
                                                     in_=x_in[0:1, 0:16]),
                        inc=sem, inc_amt=16)

        consts = [0.0, 1.0, SM_SCALE, ESHIFT]
        for i, cv in enumerate(consts):
            P.op("gpsimd", (lambda i=i, cv=cv: (lambda e: e.memset(cst[:, i:i + 1], cv)))())
            nc.const_aps.register(F32, cv, cst[:, i:i + 1])
        P.op("gpsimd", lambda e: e.memset(ones[:], 1.0))
        P.op("gpsimd", lambda e: e.memset(onesr[:], 1.0))
        P.op("gpsimd", lambda e: e.memset(eye[:], 0.0))
        setup_done = P.op("gpsimd", lambda e: e.affine_select(
            out=eye[:], in_=eye[:], compare_op=ALU.not_equal, fill=1.0,
            base=0, pattern=[[-1, 128]], channel_multiplier=1), inc=sPO)

        for dst, src in [(bqkv, qkvb_t), (bob, ob_t_in), (bfb, fb_t_in),
                         (bsc, wsc_in), (bzp, wzp_in)]:
            P.op("sync", (lambda dst=dst, src=src: (lambda e: e.dma_start(out=dst[:], in_=src)))(),
                 inc=sL, inc_amt=16)
        bias_ld = echo("sync", sL)

        def emit_ln_stats_apply(xsrc, ydst, ready_waits, apply_extra_waits=()):
            vlast = None
            for i in range(8):
                w = ready_waits if i == 0 else ()
                vlast = P.op("vector", (lambda i=i, xsrc=xsrc: (lambda e: e.bn_stats(
                    stt[:, i * 6:(i + 1) * 6], xsrc[:, i * 512:(i + 1) * 512])))(),
                    waits=w, inc=sVE)
            vlast = P.op("vector", lambda e: e.bn_aggr(mv[:, 0:2], stt[:, 0:48]),
                         waits=[(sVE, vlast)], inc=sVE)
            ve = P.op("vector", lambda e: e.tensor_single_scalar(mv[:, 4:5], mv[:, 1:2], EPS, ALU.add),
                      waits=[(sVE, vlast)], inc=sVE)
            ae = P.op("scalar", lambda e: e.activation(mv[:, 5:6], mv[:, 4:5], AF.Sqrt,
                                                       bias=0.0, scale=1.0),
                      waits=[(sVE, ve), (sPO, setup_done)], inc=sAE)
            vr = P.op("vector", lambda e: e.reciprocal(mv[:, 2:3], mv[:, 5:6]),
                      waits=[(sAE, ae)], inc=sVE)
            ve = P.op("vector", lambda e: e.tensor_scalar(mv[:, 3:4], mv[:, 0:1],
                                                          mv[:, 2:3], -1.0, ALU.mult, ALU.mult),
                      waits=[(sVE, vr)], inc=sVE)
            ae = P.op("scalar", (lambda xsrc=xsrc, ydst=ydst: (lambda e: e.activation(
                ydst[:], xsrc[:], AF.Identity, bias=mv[:, 3:4], scale=mv[:, 2:3])))(),
                waits=[(sVE, ve)] + list(apply_extra_waits), inc=sAE)
            return ae

        tr_state = {"last_evac": {}, "mm": None}

        def emit_transpose_to_big(ysrc, tt, first_waits):
            last_pe = None
            last_ve = None
            for g in range(8):
                bank = pC if g % 2 == 0 else pD
                for j in range(4):
                    fc = g * 4 + j
                    w = []
                    if fc == 0:
                        w += list(first_waits)
                    if j == 0:
                        prev = tr_state["last_evac"].get(g % 2)
                        if prev:
                            w.append((sVE, prev))
                    inc = sPE if j == 3 else None
                    r = P.op("tensor", (lambda bank=bank, j=j, fc=fc, ysrc=ysrc: (lambda e: e.matmul(
                        bank[:, j * 128:(j + 1) * 128], ysrc[:, fc * 128:(fc + 1) * 128],
                        eye[:], start=True, stop=True)))(), waits=w, inc=inc)
                    if inc:
                        last_pe = r
                def mkevac(bank=bank, g=g, tt=tt):
                    def f(e):
                        dst = big[:].rearrange("p (f t) -> p f t", t=TOK)[
                            :, g * 4:(g + 1) * 4, tt * 128:(tt + 1) * 128]
                        src = bank[:].rearrange("p (f j) -> p f j", j=128)
                        return e.tensor_copy(dst, src)
                    return f
                last_ve = P.op("vector", mkevac(), waits=[(sPE, last_pe)], inc=sVE)
                tr_state["last_evac"][g % 2] = last_ve
            last_ve = P.op("scalar", (lambda tt=tt: (lambda e: e.activation(
                big[:].rearrange("p (f t) -> p f t", t=TOK)[:, :, tt * 128:(tt + 1) * 128],
                big[:].rearrange("p (f t) -> p f t", t=TOK)[:, :, tt * 128:(tt + 1) * 128],
                AF.Identity, bias=0.0, scale=1.0)))(),
                waits=[(sVE, last_ve), (sPO, setup_done)], inc=sVE)
            return last_pe, last_ve

        p1_apply = {}
        p1_ve_done = None
        for tt in range(NT):
            xb = bufC if tt % 2 == 0 else bufD
            w = [(sL, bias_ld)] if tt == 0 else []
            if tt >= 2:
                w.append((sAE, p1_apply[tt - 2]))
            xl = P.op("sync", (lambda tt=tt, xb=xb: (lambda e: e.dma_start(
                out=xb[:], in_=x_in[tt * 128:(tt + 1) * 128, :])))(),
                waits=w, inc=sL, inc_amt=16)
            xl = echo("sync", sL)
            aw = [(sPE, tr_state["mm"])] if tr_state["mm"] else []
            ae = emit_ln_stats_apply(xb, wnk, ready_waits=[(sL, xl)], apply_extra_waits=aw)
            p1_apply[tt] = ae
            if DEBUG and tt == 0:
                P.op("scalar", lambda e: e.dma_start(out=dbg["d_wnk0"], in_=wnk[:]),
                     waits=[(sAE, ae)], inc=sS, inc_amt=16)
                P.op("scalar", lambda e: e.dma_start(out=dbg["d_mv0"], in_=mv[:]),
                     waits=[(sAE, ae)], inc=sS, inc_amt=16)
            mmn, p1_ve_done = emit_transpose_to_big(wnk, tt, first_waits=[(sAE, ae)])
            tr_state["mm"] = mmn
        p1_done_ve = p1_ve_done
        if DEBUG:
            P.op("scalar", lambda e: e.dma_start(out=dbg["d_y1t"], in_=big[:]),
                 waits=[(sVE, p1_done_ve)], inc=sS, inc_amt=16)

        BANKS8 = [pA[:, 0:512], pA[:, 512:1024], pB[:, 0:512], pB[:, 512:1024],
                  pC[:, :], pD[:, :], pE[:, :], pF[:, :]]
        BANKS4 = BANKS8[:4]

        wsn = {"cast": {}}
        fcuse = {}
        evac_ae = {}
        stg_store = {}
        gidx_counter = [0]
        wdma = {}

        def weight_stream(wt, col0, gidx, fc, cast_extra=()):
            slot = (gidx * 32 + fc) % 3
            w = []
            prevc = wsn["cast"].get((gidx * 32 + fc) - 3)
            if prevc:
                w.append((sPO, prevc))
            ld = P.op("sync", (lambda wt=wt, col0=col0, fc=fc, slot=slot: (lambda e: e.dma_start(
                out=wstg[:, slot * 256:(slot + 1) * 256],
                in_=wt[fc * 128:(fc + 1) * 128, col0:col0 + 256])))(),
                waits=w, inc=sL, inc_amt=16)
            wdma[(gidx, fc)] = ld
            w2 = [(sL, ld + 16)] + list(cast_extra)
            pu = fcuse.get((gidx - 1, fc))
            if pu:
                w2.append((sPE, pu))
            c = P.op("scalar", (lambda fc=fc, slot=slot: (lambda e: e.activation(
                dbuf[:, fc * 256:(fc + 1) * 256], wstg[:, slot * 256:(slot + 1) * 256],
                AF.Identity, bias=0.0, scale=1.0)))(),
                waits=w2 + [(sPO, setup_done)], inc=sPO)
            wsn["cast"][gidx * 32 + fc] = c

        def qkv_group(of0, kind, extra_first_waits=(), cast_extra=(), last_in_segment=False):
            gidx = gidx_counter[0]
            gidx_counter[0] += 1
            wt = wo if kind == "wo" else wqkv
            col0 = of0 * 128
            for fc in range(NFC):
                weight_stream(wt, col0, gidx, fc, cast_extra=cast_extra)
            if last_in_segment:
                echo("sync", sL)
            act_src = big

            def mm_of_fc(fc, j, sl, bank, start, stop, extra_w=(), inc=None):
                return P.op("tensor", (lambda fc=fc, j=j, sl=sl, bank=bank, start=start, stop=stop:
                            (lambda e: e.matmul(bank, dbuf[:, fc * 256 + j * 128: fc * 256 + (j + 1) * 128],
                                                fm(act_src, fc, sl * 512, (sl + 1) * 512),
                                                start=start, stop=stop)))(),
                            waits=list(extra_w), inc=inc)

            if kind in ("k", "q"):
                passes = [(0, 1, 2, 3)]
                banksel = lambda j, si: BANKS8[j * 4 + si]
            else:
                passes = [(0, 1), (2, 3)]
                banksel = lambda j, si: BANKS4[j * 2 + si]

            for pi, sls in enumerate(passes):
                for fc in range(NFC):
                    first_extra = []
                    if pi == 0:
                        first_extra.append((sPO, wsn["cast"][gidx * 32 + fc]))
                    if fc == 0:
                        if pi > 0:
                            pe_prev = evac_ae.get((gidx, pi - 1))
                        else:
                            pe_prev = evac_ae.get((gidx - 1, "last"))
                        if pe_prev:
                            first_extra.append((sAE, pe_prev))
                        if gidx == 0:
                            first_extra.append((sVE, p1_done_ve))
                        first_extra.extend(extra_first_waits)
                        extra_first_waits = ()
                    nmm = len(sls) * 2
                    cnt = 0
                    for j in range(2):
                        for si, sl in enumerate(sls):
                            cnt += 1
                            inc = sPE if cnt == nmm else None
                            r = mm_of_fc(fc, j, sl, banksel(j, si),
                                         start=(fc == 0), stop=(fc == NFC - 1),
                                         extra_w=first_extra if cnt == 1 else (), inc=inc)
                    fcuse[(gidx, fc)] = r
                last_pe = fcuse[(gidx, NFC - 1)]
                for j in range(2):
                    for si, sl in enumerate(sls):
                        of = of0 + j
                        slotid = (j * len(sls) + si) if kind in ("v", "wo") else (j * 4 + si)
                        w = [(sPE, last_pe)]
                        pv = stg_store.get(slotid)
                        if pv:
                            w.append((sS, pv))
                        bias = bqkv[:, of:of + 1] if kind != "wo" else bob[:, of:of + 1]
                        ae = P.op("scalar", (lambda bank=banksel(j, si), slotid=slotid, bias=bias:
                                  (lambda e: e.activation(stg[:, slotid * 512:(slotid + 1) * 512],
                                                          bank, AF.Identity, bias=bias, scale=1.0)))(),
                                  waits=w, inc=sAE)
                        if kind in ("k", "q"):
                            base = q_scr if kind == "q" else kcc_in
                            r0 = (of - (32 if kind == "k" else 0)) * 128
                            sc = P.op("scalar", (lambda base=base, r0=r0, sl=sl, slotid=slotid:
                                      (lambda e: e.dma_start(
                                          out=base[r0:r0 + 128, sl * 512:(sl + 1) * 512],
                                          in_=stg[:, slotid * 512:(slotid + 1) * 512])))(),
                                      waits=[(sAE, ae)], inc=sS, inc_amt=16)
                            stg_store[slotid] = sc
                        else:
                            tb = pE if (j + si) % 2 == 0 else pF
                            tslot = 4 + (j * len(sls) + si) % 4
                            tpe = None
                            for i4 in range(4):
                                wv = [(sAE, ae)] if i4 == 0 else []
                                pvv = tr_state["last_evac"].get(("p2t", (j + si) % 2))
                                if i4 == 0 and pvv:
                                    wv.append((sVE, pvv))
                                tpe = P.op("tensor", (lambda tb=tb, i4=i4, slotid=slotid:
                                           (lambda e: e.matmul(tb[:, i4 * 128:(i4 + 1) * 128],
                                                               stg[:, slotid * 512 + i4 * 128: slotid * 512 + (i4 + 1) * 128],
                                                               eye[:], start=True, stop=True)))(),
                                           waits=wv, inc=(sPE if i4 == 3 else None))
                            wv2 = [(sPE, tpe)]
                            pv2 = stg_store.get(tslot)
                            if pv2:
                                wv2.append((sS, pv2))
                            ve = P.op("vector", (lambda tb=tb, tslot=tslot: (lambda e: e.tensor_copy(
                                stg[:, tslot * 512:(tslot + 1) * 512], tb[:])))(),
                                waits=wv2, inc=sVE)
                            tr_state["last_evac"][("p2t", (j + si) % 2)] = ve
                            for i4 in range(4):
                                tt = sl * 4 + i4
                                if kind == "v":
                                    blk = vcc_in[(of - 64) * 16 + tt]
                                else:
                                    blk = attn_scr[tt * 32 + of]
                                sc = P.op("scalar", (lambda blk=blk, tslot=tslot, i4=i4:
                                          (lambda e: e.dma_start(
                                              out=blk,
                                              in_=stg[:, tslot * 512 + i4 * 128: tslot * 512 + (i4 + 1) * 128])))(),
                                          waits=[(sVE, ve)], inc=sS, inc_amt=16)
                            stg_store[tslot] = sc
                evac_ae[(gidx, pi)] = sAE.n
            evac_ae[(gidx, "last")] = sAE.n
            return gidx

        for gof in range(16):
            qkv_group(64 + gof * 2, "v")
        v_stores_done = sS.n
        cc_v = None
        for gof in range(16):
            qkv_group(32 + gof * 2, "k")
        k_stores_done = sS.n
        cc_v = None
        cc_k = None
        for gof in range(16):
            qkv_group(gof * 2, "q", last_in_segment=(gof == 15))
        q_stores_done = sS.n
        cc_v = P.op("gpsimd", lambda e: e.collective_compute(
            "AllGather", ALU.bypass,
            replica_groups=[[0, 1], [2, 3], [4, 5], [6, 7]],
            ins=[vcc_in_t.ap().opt()], outs=[vcc_out_t.ap().opt()]),
            waits=[(sS, q_stores_done)], inc=sCC)
        cc_k = P.op("gpsimd", lambda e: e.collective_compute(
            "AllGather", ALU.bypass,
            replica_groups=[[0, 1], [2, 3], [4, 5], [6, 7]],
            ins=[kcc_in_t.ap().opt()], outs=[kcc_out_t.ap().opt()]),
            waits=[(sS, q_stores_done)], inc=sCC)
        p2_last_pe = sPE.n
        p2_last_ae = sAE.n

        if DEBUG:
            P.op("sync", lambda e: e.dma_start(out=dbg["d_q"], in_=q_scr),
                 waits=[(sS, q_stores_done)], inc=sL, inc_amt=16)
            P.op("sync", lambda e: e.dma_start(out=dbg["d_k"], in_=kcc_in),
                 waits=[(sS, k_stores_done)], inc=sL, inc_amt=16)
            P.op("sync", lambda e: e.dma_start(out=dbg["d_v"], in_=vcc_in),
                 waits=[(sS, v_stores_done)], inc=sL, inc_amt=16)
            P.op("sync", lambda e: e.dma_start(out=dbg["d_kg"], in_=kcc_out),
                 waits=[(sCC, cc_k)], inc=sL, inc_amt=16)

        kv_ld = {}
        s_done = {}
        exp_done = {}
        ctx_done = {}
        smm_last = {}
        ctx_evac = {}
        head_last_mm = {}
        bc_copy = {}
        qld = {}

        for h in range(HEADS):
            w = [(sCC, cc_k), (sS, q_stores_done)]
            if h == 0:
                w.append((sPE, p2_last_pe))
            else:
                w.append((sPE, head_last_mm[h - 1]))
            for r in range(2):
                P.op("sync", (lambda h=h, r=r: (lambda e: e.dma_start(
                    out=dbuf[:, r * 2048:(r + 1) * 2048],
                    in_=kcc_out[r * DM + h * 128: r * DM + (h + 1) * 128, :])))(),
                    waits=w if r == 0 else [], inc=sL, inc_amt=16)
            for r in range(2):
                P.op("sync", (lambda h=h, r=r: (lambda e: e.dma_start(
                    out=dbuf[:, 4096 + r * 2048: 4096 + (r + 1) * 2048].rearrange(
                        "p (t j) -> p t j", j=128),
                    in_=vcc_out[r * 512 + h * 16: r * 512 + (h + 1) * 16].rearrange(
                        "t p j -> p t j"))))(),
                    waits=[(sCC, cc_v)] if r == 0 else [], inc=sL, inc_amt=16)
            kv_ld[h] = echo("sync", sL)

            for qt in range(4):
                i = h * 4 + qt
                wq = []
                if i >= 2 and smm_last.get(i - 2):
                    wq.append((sPE, smm_last[i - 2]))
                P.op("sync", (lambda h=h, qt=qt, i=i: (lambda e: e.dma_start(
                    out=qbuf[:, (i % 2) * 512:(i % 2) * 512 + 512],
                    in_=q_scr[h * 128:(h + 1) * 128, qt * 512:(qt + 1) * 512])))(),
                    waits=wq, inc=sL, inc_amt=16)
                qld[i] = echo("sync", sL)

                ctxbank = pC if i % 2 == 0 else pD
                lbank = pE if i % 2 == 0 else pF
                for jp in range(16):
                    par = jp % 2
                    sb_ = pA if par == 0 else pB
                    gpair = i * 16 + jp
                    w1 = []
                    pe_prev = exp_done.get(gpair - 2)
                    if pe_prev:
                        w1.append((sAE, pe_prev))
                    if jp == 0:
                        w1 += [(sL, qld[i]), (sL, kv_ld[h])]
                        pv = ctx_evac.get(i - 2)
                        if pv:
                            w1.append((sVE, pv))
                        pv = bc_copy.get(i - 1)
                        if pv:
                            w1.append((sVE, pv))
                        if i == 0:
                            w1.append((sPO, setup_done))
                    for half in range(2):
                        kc = jp * 2 + half
                        rsel, idx = kc // 16, kc % 16
                        r = P.op("tensor", (lambda sb_=sb_, half=half, rsel=rsel, idx=idx, i=i:
                                 (lambda e: e.matmul(
                                     sb_[:, half * 512:(half + 1) * 512],
                                     dbuf[:, rsel * 2048 + idx * 128: rsel * 2048 + (idx + 1) * 128],
                                     qbuf[:, (i % 2) * 512:(i % 2) * 512 + 512],
                                     start=True, stop=True)))(),
                                 waits=w1 if half == 0 else (), inc=(sPE if half == 1 else None))
                    s_done[gpair] = r
                    smm_last[i] = r
                    w2 = [(sPE, s_done[gpair])]
                    pv = ctx_done.get(gpair - 2)
                    if pv:
                        w2.append((sPE, pv))
                    exp_done[gpair] = P.op("scalar", (lambda sb_=sb_, par=par: (lambda e: e.activation(
                        pbuf[:, par * 1024:(par + 1) * 1024], sb_[:],
                        AF.Exp, bias=ESHIFT, scale=SM_SCALE)))(),
                        waits=w2, inc=sAE)
                    w3 = [(sAE, exp_done[gpair])]
                    lastmm = None
                    for half in range(2):
                        kc = jp * 2 + half
                        rsel, idx = kc // 16, kc % 16
                        P.op("tensor", (lambda ctxbank=ctxbank, par=par, half=half, rsel=rsel, idx=idx, kc=kc:
                             (lambda e: e.matmul(
                                 ctxbank[:], dbuf[:, 4096 + rsel * 2048 + idx * 128: 4096 + rsel * 2048 + (idx + 1) * 128],
                                 pbuf[:, par * 1024 + half * 512: par * 1024 + (half + 1) * 512],
                                 start=(kc == 0), stop=(kc == 31))))(),
                             waits=w3 if half == 0 else ())
                    for half in range(2):
                        kc = jp * 2 + half
                        lastmm = P.op("tensor", (lambda lbank=lbank, par=par, half=half, kc=kc:
                                      (lambda e: e.matmul(
                                          lbank[0:1, :], ones[:],
                                          pbuf[:, par * 1024 + half * 512: par * 1024 + (half + 1) * 512],
                                          start=(kc == 0), stop=(kc == 31))))(),
                                      inc=(sPE if half == 1 else None))
                    ctx_done[gpair] = lastmm
                vr = P.op("vector", (lambda lbank=lbank: (lambda e: e.reciprocal(
                    rrowf[:], lbank[0:1, :])))(), waits=[(sPE, ctx_done[i * 16 + 15])], inc=sVE)
                ve1 = P.op("scalar", lambda e: e.activation(rrow[:], rrowf[:],
                           AF.Identity, bias=0.0, scale=1.0),
                           waits=[(sVE, vr)], inc=sVE)
                wb = [(sVE, ve1), (sAE, exp_done[i * 16 + 14])]
                bpe = P.op("tensor", lambda e: e.matmul(
                    pA[:, 0:512], onesr[:], rrow[:], start=True, stop=True),
                    waits=wb, inc=sPE)
                ve2 = P.op("vector", lambda e: e.tensor_copy(bcast[:], pA[:, 0:512]),
                           waits=[(sPE, bpe)], inc=sVE)
                bc_copy[i] = ve2
                cx = P.op("vector", (lambda ctxbank=ctxbank, h=h, qt=qt: (lambda e: e.tensor_mul(
                    big[:, h * TOK + qt * 512: h * TOK + (qt + 1) * 512],
                    ctxbank[:], bcast[:])))(), waits=[(sVE, ve2)], inc=sVE)
                ctx_evac[i] = P.op("vector", (lambda h=h, qt=qt: (lambda e: e.tensor_copy(
                    cnf[:], big[:, h * TOK + qt * 512 + 496: h * TOK + qt * 512 + 512])))(),
                    waits=[(sVE, cx)], inc=sVE)
                head_last_mm[h] = ctx_done[i * 16 + 15]
        p3_done_ve = sVE.n
        p3_done_pe = sPE.n

        if DEBUG:
            P.op("scalar", lambda e: e.dma_start(out=dbg["d_ctx"], in_=big[:]),
                 waits=[(sVE, p3_done_ve)], inc=sS, inc_amt=16)

        p3_seal = P.op("scalar", lambda e: e.activation(
            big[:, 31 * TOK + 3 * 512: 31 * TOK + 4 * 512],
            big[:, 31 * TOK + 3 * 512: 31 * TOK + 4 * 512],
            AF.Identity, bias=0.0, scale=1.0), waits=[(sVE, sVE.n)], inc=sAE)
        p3_exp_last = sAE.n
        for gof in range(16):
            qkv_group(gof * 2, "wo",
                      extra_first_waits=((sVE, p3_done_ve), (sAE, p3_exp_last)) if gof == 0 else (),
                      cast_extra=((sPE, p3_done_pe),) if gof == 0 else (),
                      last_in_segment=(gof == 15))
        p4_stores_done = sS.n
        p4_last_pe = sPE.n

        if DEBUG:
            P.op("sync", lambda e: e.dma_start(out=dbg["d_attn"], in_=attn_scr),
                 waits=[(sS, p4_stores_done)], inc=sL, inc_amt=16)

        x2_store = {}
        ln2_ae = {}
        ln3_ae = {}
        add_ve = {}
        for tt in range(NT):
            wa = [(sS, p4_stores_done)] if tt == 0 else [(sAE, ln2_ae[tt - 1])]
            al = P.op("sync", (lambda tt=tt: (lambda e: e.dma_start(
                out=stg[:].rearrange("p (t j) -> p t j", j=128),
                in_=attn_scr[tt * 32:(tt + 1) * 32].rearrange("t p j -> p t j"))))(),
                waits=wa, inc=sL, inc_amt=16)
            al = echo("sync", sL)
            wx = []
            if tt >= 1:
                wx.append((sAE, ln3_ae[tt - 1]))
                wx.append((sS, x2_store[tt - 1]))
            xl = P.op("sync", (lambda tt=tt: (lambda e: e.dma_start(
                out=bufC[:], in_=x_in[tt * 128:(tt + 1) * 128, :])))(),
                waits=wx, inc=sL, inc_amt=16)
            xl = echo("sync", sL)
            wd = [(sVE, add_ve[tt - 1])] if tt >= 1 else []
            ln2_ae[tt] = emit_ln_stats_apply(stg, bufD, ready_waits=[(sL, al)],
                                             apply_extra_waits=wd)
            add_ve[tt] = P.op("vector", lambda e: e.tensor_add(bufC[:], bufC[:], bufD[:]),
                              waits=[(sAE, ln2_ae[tt]), (sL, xl)], inc=sVE)
            x2_store[tt] = P.op("scalar", (lambda tt=tt: (lambda e: e.dma_start(
                out=x2_scr[tt * 128:(tt + 1) * 128, :], in_=bufC[:])))(),
                waits=[(sVE, add_ve[tt])], inc=sS, inc_amt=16)
            aw = [(sPE, tr_state["mm"])] if tr_state["mm"] else []
            ln3_ae[tt] = emit_ln_stats_apply(bufC, wnk, ready_waits=[(sVE, add_ve[tt])],
                                             apply_extra_waits=aw)
            first = [(sAE, ln3_ae[tt])]
            if tt == 0:
                first.append((sPE, p4_last_pe))
            mmn, _ = emit_transpose_to_big(wnk, tt, first_waits=first)
            tr_state["mm"] = mmn
        p5_done_ve = sVE.n
        p5_last_ae = sAE.n
        x2_all = sS.n

        if DEBUG:
            P.op("scalar", lambda e: e.dma_start(out=dbg["d_x2"], in_=x2_scr),
                 waits=[(sS, x2_all)], inc=sS, inc_amt=16)

        qw_ld = {}
        deq_done = {}
        wT_evac = {}
        ffn_mm_done = {}
        ffn_stores = {}
        for n in range(NFC):
            w = []
            if n == 0:
                w = [(sS, x2_all), (sAE, p5_last_ae), (sVE, p5_done_ve)]
            else:
                w = [(sVE, deq_done[n - 1])]
            P.op("gpsimd", (lambda n=n: (lambda e: e.dma_start(
                out=bufC[:, 0:3072], in_=qwq[n * 128:(n + 1) * 128, :])))(),
                waits=w, inc=sG, inc_amt=16)
            qw_ld[n] = echo("gpsimd", sG)
            wv = [(sG, qw_ld[n])]
            if tr_state["mm"]:
                wv.append((sPE, tr_state["mm"]))
            P.op("vector", lambda e: e.tensor_scalar(
                bufD[:, 0:3072], bufC[:, 0:3072], 1.0 / 64.0, -0.4921875,
                ALU.mult, ALU.add), waits=wv)
            P.op("vector", lambda e: e.tensor_scalar(
                bufD[:, 0:3072], bufD[:, 0:3072], C_ROUND, -C_ROUND,
                ALU.add, ALU.add))
            P.op("vector", lambda e: e.scalar_tensor_tensor(
                bufC[:, 0:3072], bufD[:, 0:3072], -64.0, bufC[:, 0:3072],
                ALU.mult, ALU.add))

            vm = P.op("vector", (lambda n=n: (lambda e: e.tensor_scalar(
                mv[:, 6:7], bzp[:, n:n + 1], bsc[:, n:n + 1], -1.0,
                ALU.mult, ALU.mult)))(), inc=sVE)

            def wv_(src, j, outj, n=n):
                sv = src[:, 0:3072].rearrange("p (g w) -> p g w", w=3)[:, :, j]
                ov = wnk[:, 0:DM].rearrange("p (g f) -> p g f", f=4)[:, :, outj]
                def f(e):
                    return e.activation(ov, sv, AF.Identity,
                                        bias=mv[:, 6:7], scale=bsc[:, n:n + 1])
                return f
            P.op("scalar", wv_(bufC, 0, 0), waits=[(sVE, vm)])
            P.op("scalar", wv_(bufD, 0, 1))
            P.op("scalar", wv_(bufC, 1, 2))
            deq_done[n] = P.op("scalar", wv_(bufC, 2, 3), inc=sVE)

            tslot = n % 2
            last_pe = None
            for g in range(8):
                bank = pC if g % 2 == 0 else pD
                for j in range(4):
                    fc = g * 4 + j
                    w = []
                    if fc == 0:
                        w.append((sVE, deq_done[n]))
                        if n >= 2 and ffn_mm_done.get(n - 2):
                            w.append((sPE, ffn_mm_done[n - 2]))
                        if n == 0:
                            w.append((sPE, p4_last_pe))
                    if j == 0:
                        pv = tr_state["last_evac"].get(("p6", g % 2))
                        if pv:
                            w.append((sVE, pv))
                    last_pe = P.op("tensor", (lambda bank=bank, j=j, fc=fc: (lambda e: e.matmul(
                        bank[:, j * 128:(j + 1) * 128], wnk[:, fc * 128:(fc + 1) * 128],
                        eye[:], start=True, stop=True)))(),
                        waits=w, inc=(sPE if j == 3 else None))
                ve = P.op("scalar", (lambda bank=bank, g=g, tslot=tslot: (lambda e: e.activation(
                    dbuf[:, tslot * 4096 + g * 512:tslot * 4096 + (g + 1) * 512], bank[:],
                    AF.Identity, bias=0.0, scale=1.0)))(),
                    waits=[(sPE, last_pe)], inc=sVE)
                tr_state["last_evac"][("p6", g % 2)] = ve
            wT_evac[n] = ve
            tr_state["mm"] = last_pe

            last = None
            for kc in range(NFC):
                for sl in range(4):
                    w = []
                    if kc == 0 and sl == 0:
                        w.append((sVE, wT_evac[n]))
                        pv = ffn_stores.get(n - 1)
                        if pv:
                            w.append((sAE, ffn_stores[n - 1][0]))
                    last = P.op("tensor", (lambda kc=kc, sl=sl, tslot=tslot: (lambda e: e.matmul(
                        BANKS4[sl], dbuf[:, tslot * 4096 + kc * 128: tslot * 4096 + (kc + 1) * 128],
                        fm(big, kc, sl * 512, (sl + 1) * 512),
                        start=(kc == 0), stop=(kc == NFC - 1))))(),
                        waits=w, inc=(sPE if (kc == NFC - 1 and sl == 3) else None))
            ffn_mm_done[n] = last

            ae_last = None
            for sl in range(4):
                w = [(sPE, ffn_mm_done[n])]
                pv = stg_store.get(("p6", sl))
                if pv:
                    w.append((sS, pv))
                ae = P.op("scalar", (lambda sl=sl, n=n: (lambda e: e.activation(
                    stg[:, sl * 512:(sl + 1) * 512], BANKS4[sl], AF.Identity,
                    bias=bfb[:, n:n + 1], scale=1.0)))(), waits=w, inc=sAE)
                tb = pE if sl % 2 == 0 else pF
                tpe = None
                for i4 in range(4):
                    w2 = [(sAE, ae)] if i4 == 0 else []
                    pv2 = tr_state["last_evac"].get(("p6o", sl % 2))
                    if i4 == 0 and pv2:
                        w2.append((sVE, pv2))
                    tpe = P.op("tensor", (lambda tb=tb, i4=i4, sl=sl: (lambda e: e.matmul(
                        tb[:, i4 * 128:(i4 + 1) * 128],
                        stg[:, sl * 512 + i4 * 128: sl * 512 + (i4 + 1) * 128],
                        eye[:], start=True, stop=True)))(),
                        waits=w2, inc=(sPE if i4 == 3 else None))
                w3 = [(sPE, tpe)]
                pv3 = stg_store.get(("p6b", sl))
                if pv3:
                    w3.append((sS, pv3))
                ve = P.op("vector", (lambda tb=tb, sl=sl: (lambda e: e.tensor_copy(
                    stg[:, (4 + sl) * 512:(5 + sl) * 512], tb[:])))(),
                    waits=w3, inc=sVE)
                tr_state["last_evac"][("p6o", sl % 2)] = ve
                sc = None
                for i4 in range(4):
                    tt = sl * 4 + i4
                    sc = P.op("scalar", (lambda tt=tt, n=n, sl=sl, i4=i4: (lambda e: e.dma_start(
                        out=ffn_scr[tt * 32 + n],
                        in_=stg[:, (4 + sl) * 512 + i4 * 128: (4 + sl) * 512 + (i4 + 1) * 128])))(),
                        waits=[(sVE, ve)], inc=sS, inc_amt=16)
                stg_store[("p6", sl)] = sS.n
                stg_store[("p6b", sl)] = sc
                ae_last = ae
            ffn_stores[n] = (ae_last, sS.n)
        p6_stores_done = sS.n

        if DEBUG:
            P.op("sync", lambda e: e.dma_start(out=dbg["d_ffn"], in_=ffn_scr),
                 waits=[(sS, p6_stores_done)], inc=sL, inc_amt=16)

        ln4_ae = {}
        out_store = {}
        add7 = {}
        for tt in range(NT):
            wa = [(sS, p6_stores_done)] if tt == 0 else [(sAE, ln4_ae[tt - 1])]
            fl = P.op("sync", (lambda tt=tt: (lambda e: e.dma_start(
                out=stg[:].rearrange("p (t j) -> p t j", j=128),
                in_=ffn_scr[tt * 32:(tt + 1) * 32].rearrange("t p j -> p t j"))))(),
                waits=wa, inc=sL, inc_amt=16)
            fl = echo("sync", sL)
            wx = []
            if tt >= 1:
                wx.append((sVE, add7[tt - 1]))
            xl = P.op("sync", (lambda tt=tt: (lambda e: e.dma_start(
                out=bufC[:], in_=x2_scr[tt * 128:(tt + 1) * 128, :])))(),
                waits=wx, inc=sL, inc_amt=16)
            xl = echo("sync", sL)
            wd = [(sS, out_store[tt - 1])] if tt >= 1 else []
            ln4_ae[tt] = emit_ln_stats_apply(stg, bufD, ready_waits=[(sL, fl)],
                                             apply_extra_waits=wd)
            add7[tt] = P.op("vector", lambda e: e.tensor_add(bufD[:], bufC[:], bufD[:]),
                            waits=[(sAE, ln4_ae[tt]), (sL, xl)], inc=sVE)
            out_store[tt] = P.op("scalar", (lambda tt=tt: (lambda e: e.dma_start(
                out=out[tt * 128:(tt + 1) * 128, :], in_=bufD[:])))(),
                waits=[(sVE, add7[tt])], inc=sS, inc_amt=16)

        final_sS = sS.n
        P.raw("sync", lambda e: e.wait_ge(sS.h, final_sS))
        P.raw("gpsimd", lambda e: e.wait_ge(sS.h, final_sS))

        with nc.Block() as block:
            @block.sync
            def _(e):
                for f in P.p["sync"]:
                    f(e)

            @block.gpsimd
            def _(e):
                for f in P.p["gpsimd"]:
                    f(e)

            @block.tensor
            def _(e):
                for f in P.p["tensor"]:
                    f(e)

            @block.vector
            def _(e):
                for f in P.p["vector"]:
                    f(e)

            @block.scalar
            def _(e):
                for f in P.p["scalar"]:
                    f(e)
    return nc


_NC_CACHE = [None]


def kernel(L0_attn_x, L0_attn_attn_qkvw, L0_attn_attn_qkvb, L0_attn_attn_ow,
           L0_attn_attn_ob, L0_ffn_qweight, L0_ffn_weight_scale,
           L0_ffn_weight_zero_point, L0_ffn_bias):
    kernel._LAST = None
    x = np.ascontiguousarray(np.asarray(L0_attn_x, dtype=np.float32))
    wqkv = np.ascontiguousarray(np.asarray(L0_attn_attn_qkvw, dtype=np.float32))
    qkvb = np.asarray(L0_attn_attn_qkvb, dtype=np.float32)
    wo = np.ascontiguousarray(np.asarray(L0_attn_attn_ow, dtype=np.float32))
    ob = np.asarray(L0_attn_attn_ob, dtype=np.float32)
    qw = np.ascontiguousarray(np.asarray(L0_ffn_qweight, dtype=np.int32))
    wsc = np.asarray(L0_ffn_weight_scale, dtype=np.float32).reshape(-1)
    wzp = np.asarray(L0_ffn_weight_zero_point, dtype=np.float32).reshape(-1)
    fb = np.asarray(L0_ffn_bias, dtype=np.float32)

    qkvb_t = np.ascontiguousarray(qkvb.reshape(NOF, 128).T)
    ob_t = np.ascontiguousarray(ob.reshape(NFC, 128).T)
    fb_t = np.ascontiguousarray(fb.reshape(NFC, 128).T)
    wsc_t = np.ascontiguousarray(wsc.reshape(NFC, 128).T)
    wzp_t = np.ascontiguousarray(wzp.reshape(NFC, 128).T)

    if _NC_CACHE[0] is None:
        _NC_CACHE[0] = build()
    nc = _NC_CACHE[0]

    in_maps = []
    for c in range(8):
        b, half = c // 2, c % 2
        xs = np.ascontiguousarray(x[b, half * TOK:(half + 1) * TOK, :])
        in_maps.append({
            "x": xs, "wqkv": wqkv, "wo": wo, "qw": qw,
            "qkvb_t": qkvb_t, "ob_t": ob_t, "fb_t": fb_t,
            "wsc_t": wsc_t, "wzp_t": wzp_t,
        })
    res = run_bass_kernel_spmd(nc, in_maps, core_ids=list(range(8)))
    kernel._LAST = res
    outp = np.empty((4, 4096, DM), dtype=np.float32)
    for c in range(8):
        b, half = c // 2, c % 2
        outp[b, half * TOK:(half + 1) * TOK, :] = res.results[c]["out"]
    return outp


# version 09
# speedup vs baseline: 1.2815x; 1.0971x over previous
import numpy as np
import contextlib
import concourse.bass as bass
import concourse.mybir as mybir
from concourse.bass_utils import run_bass_kernel_spmd

F32 = mybir.dt.float32
BF16 = mybir.dt.bfloat16
I32 = mybir.dt.int32
AF = mybir.ActivationFunctionType
ALU = mybir.AluOpType
AX = mybir.AxisListType

DM = 4096
TOK = 2048
NT = TOK // 128
NS = TOK // 512
NFC = DM // 128
HEADS = 32
HD = 128
NOF = 96
SM_SCALE = 0.08838834764831845
ESHIFT = -16.0
EPS = 1e-5
C_ROUND = 12582912.0
DEBUG = False

ENGS = ("sync", "gpsimd", "tensor", "vector", "scalar")


class Sem:
    __slots__ = ("h", "n")

    def __init__(self, h):
        self.h = h
        self.n = 0


class Prog:
    def __init__(self):
        self.p = {e: [] for e in ENGS}

    def op(self, eng, fn, waits=(), inc=None, inc_amt=1):
        if inc is not None:
            inc.n += inc_amt
        v = inc.n if inc is not None else None
        waits = tuple((s, val) for (s, val) in waits if val and val > 0)

        def run(e, fn=fn, waits=waits, inc=inc, inc_amt=inc_amt):
            for (s, val) in waits:
                e.wait_ge(s.h, val)
            inst = fn(e)
            if inc is not None:
                inst.then_inc(inc.h, inc_amt)

        self.p[eng].append(run)
        return v

    def raw(self, eng, fn):
        self.p[eng].append(fn)


def fm(t, chunk, c0, c1):
    return t[:, chunk * TOK + c0: chunk * TOK + c1]


def build():
    nc = bass.Bass("TRN2", target_bir_lowering=False)

    x_in = nc.dram_tensor("x", [TOK, DM], F32, kind="ExternalInput").ap()
    wqkv = nc.dram_tensor("wqkv", [DM, 3 * DM], F32, kind="ExternalInput").ap()
    wo = nc.dram_tensor("wo", [DM, DM], F32, kind="ExternalInput").ap()
    qwq = nc.dram_tensor("qw", [DM, 3072], I32, kind="ExternalInput").ap()
    qkvb_t = nc.dram_tensor("qkvb_t", [128, NOF], F32, kind="ExternalInput").ap()
    ob_t_in = nc.dram_tensor("ob_t", [128, NFC], F32, kind="ExternalInput").ap()
    fb_t_in = nc.dram_tensor("fb_t", [128, NFC], F32, kind="ExternalInput").ap()
    wsc_in = nc.dram_tensor("wsc_t", [128, NFC], F32, kind="ExternalInput").ap()
    wzp_in = nc.dram_tensor("wzp_t", [128, NFC], F32, kind="ExternalInput").ap()
    out = nc.dram_tensor("out", [TOK, DM], F32, kind="ExternalOutput").ap()

    q_scr = nc.dram_tensor("q_scr", [DM, TOK], BF16, kind="Internal").ap()
    kcc_in_t = nc.dram_tensor("kcc_in", [DM, TOK], BF16, kind="Internal")
    kcc_out_t = nc.dram_tensor("kcc_out", [2 * DM, TOK], BF16, kind="Internal")
    vcc_in_t = nc.dram_tensor("vcc_in", [512, 128, 128], BF16, kind="Internal")
    vcc_out_t = nc.dram_tensor("vcc_out", [1024, 128, 128], BF16, kind="Internal")
    kcc_in, kcc_out = kcc_in_t.ap(), kcc_out_t.ap()
    vcc_in, vcc_out = vcc_in_t.ap(), vcc_out_t.ap()
    attn_scr = nc.dram_tensor("attn_scr", [512, 128, 128], BF16, kind="Internal").ap()
    ffn_scr = nc.dram_tensor("ffn_scr", [512, 128, 128], BF16, kind="Internal").ap()
    x2_scr = nc.dram_tensor("x2_scr", [TOK, DM], F32, kind="Internal").ap()

    dbg = {}
    if DEBUG:
        for nm, shp, dt in [("d_q", [DM, TOK], BF16), ("d_k", [DM, TOK], BF16),
                            ("d_v", [512, 128, 128], BF16),
                            ("d_kg", [2 * DM, TOK], BF16),
                            ("d_ctx", [128, HEADS * TOK], BF16),
                            ("d_attn", [512, 128, 128], BF16),
                            ("d_x2", [TOK, DM], F32),
                            ("d_ffn", [512, 128, 128], BF16),
                            ("d_y1t", [128, 65536], BF16),
                            ("d_wnk0", [128, DM], BF16),
                            ("d_mv0", [128, 8], F32)]:
            dbg[nm] = nc.dram_tensor(nm, shp, dt, kind="ExternalOutput").ap()

    with contextlib.ExitStack() as st:
        def sb(name, shape, dt=F32):
            return st.enter_context(nc.sbuf_tensor(name, list(shape), dt))

        def ps(name, shape, dt=F32):
            return st.enter_context(nc.psum_tensor(name, list(shape), dt))

        big = sb("big", [128, 65536], BF16)
        dbuf = sb("dbuf", [128, 8192], BF16)
        bufC = sb("bufC", [128, DM], F32)
        bufD = sb("bufD", [128, DM], F32)
        stg = sb("stg", [128, DM], BF16)
        wnk = sb("wnk", [128, DM], BF16)
        wstg = sb("wstg", [128, 1280], F32)
        pbuf = sb("pbuf", [128, 2048], BF16)
        qbuf = sb("qbuf", [128, 1024], BF16)
        eye = sb("eye", [128, 128], BF16)
        bqkv = sb("bqkv", [128, NOF], F32)
        bob = sb("bob", [128, NFC], F32)
        bfb = sb("bfb", [128, NFC], F32)
        bsc = sb("bsc", [128, NFC], F32)
        bzp = sb("bzp", [128, NFC], F32)
        stt = sb("stt", [128, 64], F32)
        mv = sb("mv", [128, 8], F32)
        cst = sb("cst", [128, 8], F32)
        ones = sb("ones", [128, 1], BF16)
        onesr = sb("onesr", [1, 128], BF16)
        rrow = sb("rrow", [1, 512], BF16)
        rrowf = sb("rrowf", [1, 512], F32)
        cnf = sb("cnf", [128, 16], BF16)

        bcast = wstg[:, 0:512]

        pA = ps("pA", [128, 1024])
        pB = ps("pB", [128, 1024])
        pC = ps("pC", [128, 512])
        pD = ps("pD", [128, 512])
        pE = ps("pE", [128, 512])
        pF = ps("pF", [128, 512])

        sL = Sem(st.enter_context(nc.semaphore("sL")))
        sS = Sem(st.enter_context(nc.semaphore("sS")))
        sG = Sem(st.enter_context(nc.semaphore("sG")))
        sCC = Sem(st.enter_context(nc.semaphore("sCC")))
        sPE = Sem(st.enter_context(nc.semaphore("sPE")))
        sVE = Sem(st.enter_context(nc.semaphore("sVE")))
        sAE = Sem(st.enter_context(nc.semaphore("sAE")))
        sPO = Sem(st.enter_context(nc.semaphore("sPO")))

        P = Prog()

        def echo(queue, sem):
            return P.op(queue, lambda e: e.dma_start(out=stt[:, 48:52],
                                                     in_=x_in[0:128, 0:4]),
                        inc=sem, inc_amt=16)

        consts = [0.0, 1.0, SM_SCALE, ESHIFT]
        for i, cv in enumerate(consts):
            P.op("gpsimd", (lambda i=i, cv=cv: (lambda e: e.memset(cst[:, i:i + 1], cv)))())
            nc.const_aps.register(F32, cv, cst[:, i:i + 1])
        P.op("gpsimd", lambda e: e.memset(ones[:], 1.0))
        P.op("gpsimd", lambda e: e.memset(onesr[:], 1.0))
        P.op("gpsimd", lambda e: e.memset(eye[:], 0.0))
        setup_done = P.op("gpsimd", lambda e: e.affine_select(
            out=eye[:], in_=eye[:], compare_op=ALU.not_equal, fill=1.0,
            base=0, pattern=[[-1, 128]], channel_multiplier=1), inc=sPO)

        for dst, src in [(bqkv, qkvb_t), (bob, ob_t_in), (bfb, fb_t_in),
                         (bsc, wsc_in), (bzp, wzp_in)]:
            P.op("sync", (lambda dst=dst, src=src: (lambda e: e.dma_start(out=dst[:], in_=src)))(),
                 inc=sL, inc_amt=16)
        bias_ld = echo("sync", sL)

        def emit_ln_stats_apply(xsrc, ydst, ready_waits, apply_extra_waits=()):
            vlast = None
            for i in range(8):
                w = ready_waits if i == 0 else ()
                vlast = P.op("vector", (lambda i=i, xsrc=xsrc: (lambda e: e.bn_stats(
                    stt[:, i * 6:(i + 1) * 6], xsrc[:, i * 512:(i + 1) * 512])))(),
                    waits=w, inc=sVE)
            vlast = P.op("vector", lambda e: e.bn_aggr(mv[:, 0:2], stt[:, 0:48]),
                         waits=[(sVE, vlast)], inc=sVE)
            ve = P.op("vector", lambda e: e.tensor_single_scalar(mv[:, 4:5], mv[:, 1:2], EPS, ALU.add),
                      waits=[(sVE, vlast)], inc=sVE)
            ae = P.op("scalar", lambda e: e.activation(mv[:, 5:6], mv[:, 4:5], AF.Sqrt,
                                                       bias=0.0, scale=1.0),
                      waits=[(sVE, ve), (sPO, setup_done)], inc=sAE)
            vr = P.op("vector", lambda e: e.reciprocal(mv[:, 2:3], mv[:, 5:6]),
                      waits=[(sAE, ae)], inc=sVE)
            ve = P.op("vector", lambda e: e.tensor_scalar(mv[:, 3:4], mv[:, 0:1],
                                                          mv[:, 2:3], -1.0, ALU.mult, ALU.mult),
                      waits=[(sVE, vr)], inc=sVE)
            ae = P.op("scalar", (lambda xsrc=xsrc, ydst=ydst: (lambda e: e.activation(
                ydst[:], xsrc[:], AF.Identity, bias=mv[:, 3:4], scale=mv[:, 2:3])))(),
                waits=[(sVE, ve)] + list(apply_extra_waits), inc=sAE)
            return ae

        tr_state = {"last_evac": {}, "mm": None}

        def emit_transpose_to_big(ysrc, tt, first_waits):
            last_pe = None
            last_ve = None
            for g in range(8):
                bank = pC if g % 2 == 0 else pD
                for j in range(4):
                    fc = g * 4 + j
                    w = []
                    if fc == 0:
                        w += list(first_waits)
                    if j == 0:
                        prev = tr_state["last_evac"].get(g % 2)
                        if prev:
                            w.append((sVE, prev))
                    inc = sPE if j == 3 else None
                    r = P.op("tensor", (lambda bank=bank, j=j, fc=fc, ysrc=ysrc: (lambda e: e.matmul(
                        bank[:, j * 128:(j + 1) * 128], ysrc[:, fc * 128:(fc + 1) * 128],
                        eye[:], start=True, stop=True)))(), waits=w, inc=inc)
                    if inc:
                        last_pe = r
                def mkevac(bank=bank, g=g, tt=tt):
                    def f(e):
                        dst = big[:].rearrange("p (f t) -> p f t", t=TOK)[
                            :, g * 4:(g + 1) * 4, tt * 128:(tt + 1) * 128]
                        src = bank[:].rearrange("p (f j) -> p f j", j=128)
                        return e.tensor_copy(dst, src)
                    return f
                last_ve = P.op("vector", mkevac(), waits=[(sPE, last_pe)], inc=sVE)
                tr_state["last_evac"][g % 2] = last_ve
            last_ve = P.op("scalar", (lambda tt=tt: (lambda e: e.activation(
                big[:].rearrange("p (f t) -> p f t", t=TOK)[:, :, tt * 128:(tt + 1) * 128],
                big[:].rearrange("p (f t) -> p f t", t=TOK)[:, :, tt * 128:(tt + 1) * 128],
                AF.Identity, bias=0.0, scale=1.0)))(),
                waits=[(sVE, last_ve), (sPO, setup_done)], inc=sVE)
            return last_pe, last_ve

        p1_apply = {}
        p1_ve_done = None
        for tt in range(NT):
            xb = bufC if tt % 2 == 0 else bufD
            w = [(sL, bias_ld)] if tt == 0 else []
            if tt >= 2:
                w.append((sAE, p1_apply[tt - 2]))
            xl = P.op("sync", (lambda tt=tt, xb=xb: (lambda e: e.dma_start(
                out=xb[:], in_=x_in[tt * 128:(tt + 1) * 128, :])))(),
                waits=w, inc=sL, inc_amt=16)
            xl = echo("sync", sL)
            aw = [(sPE, tr_state["mm"])] if tr_state["mm"] else []
            ae = emit_ln_stats_apply(xb, wnk, ready_waits=[(sL, xl)], apply_extra_waits=aw)
            p1_apply[tt] = ae
            if DEBUG and tt == 0:
                P.op("scalar", lambda e: e.dma_start(out=dbg["d_wnk0"], in_=wnk[:]),
                     waits=[(sAE, ae)], inc=sS, inc_amt=16)
                P.op("scalar", lambda e: e.dma_start(out=dbg["d_mv0"], in_=mv[:]),
                     waits=[(sAE, ae)], inc=sS, inc_amt=16)
            mmn, p1_ve_done = emit_transpose_to_big(wnk, tt, first_waits=[(sAE, ae)])
            tr_state["mm"] = mmn
        p1_done_ve = p1_ve_done
        if DEBUG:
            P.op("scalar", lambda e: e.dma_start(out=dbg["d_y1t"], in_=big[:]),
                 waits=[(sVE, p1_done_ve)], inc=sS, inc_amt=16)

        BANKS8 = [pA[:, 0:512], pA[:, 512:1024], pB[:, 0:512], pB[:, 512:1024],
                  pC[:, :], pD[:, :], pE[:, :], pF[:, :]]
        BANKS4 = BANKS8[:4]

        wsn = {"cast": {}}
        fcuse = {}
        evac_ae = {}
        stg_store = {}
        gidx_counter = [0]
        wdma = {}

        def weight_stream(wt, col0, gidx, fc, cast_extra=(), load_extra=()):
            slot = (gidx * 32 + fc) % 5
            w = []
            prevc = wsn["cast"].get((gidx * 32 + fc) - 5)
            if prevc:
                w.append((sPO, prevc))
            ld = P.op("sync", (lambda wt=wt, col0=col0, fc=fc, slot=slot: (lambda e: e.dma_start(
                out=wstg[:, slot * 256:(slot + 1) * 256],
                in_=wt[fc * 128:(fc + 1) * 128, col0:col0 + 256])))(),
                waits=w + list(load_extra), inc=sL, inc_amt=16)
            wdma[(gidx, fc)] = ld
            w2 = [(sL, ld + 16)] + list(cast_extra)
            pu = fcuse.get((gidx - 1, fc))
            if pu:
                w2.append((sPE, pu))
            c = P.op("scalar", (lambda fc=fc, slot=slot: (lambda e: e.activation(
                dbuf[:, fc * 256:(fc + 1) * 256], wstg[:, slot * 256:(slot + 1) * 256],
                AF.Identity, bias=0.0, scale=1.0)))(),
                waits=w2 + [(sPO, setup_done)], inc=sPO)
            wsn["cast"][gidx * 32 + fc] = c

        def qkv_group(of0, kind, extra_first_waits=(), cast_extra=(), last_in_segment=False,
                      load_extra=()):
            gidx = gidx_counter[0]
            gidx_counter[0] += 1
            wt = wo if kind == "wo" else wqkv
            col0 = of0 * 128
            for fc in range(NFC):
                weight_stream(wt, col0, gidx, fc, cast_extra=cast_extra,
                              load_extra=load_extra if fc < 5 else ())
            if last_in_segment:
                echo("sync", sL)
            act_src = big

            def mm_of_fc(fc, j, sl, bank, start, stop, extra_w=(), inc=None):
                return P.op("tensor", (lambda fc=fc, j=j, sl=sl, bank=bank, start=start, stop=stop:
                            (lambda e: e.matmul(bank, dbuf[:, fc * 256 + j * 128: fc * 256 + (j + 1) * 128],
                                                fm(act_src, fc, sl * 512, (sl + 1) * 512),
                                                start=start, stop=stop)))(),
                            waits=list(extra_w), inc=inc)

            if kind in ("k", "q"):
                passes = [(0, 1, 2, 3)]
                banksel = lambda j, si: BANKS8[j * 4 + si]
            else:
                passes = [(0, 1), (2, 3)]
                banksel = lambda j, si: BANKS4[j * 2 + si]

            for pi, sls in enumerate(passes):
                for fc in range(NFC):
                    first_extra = []
                    if pi == 0:
                        first_extra.append((sPO, wsn["cast"][gidx * 32 + fc]))
                    if fc == 0:
                        if pi > 0:
                            pe_prev = evac_ae.get((gidx, pi - 1))
                        else:
                            pe_prev = evac_ae.get((gidx - 1, "last"))
                        if pe_prev:
                            first_extra.append((sAE, pe_prev))
                        if gidx == 0:
                            first_extra.append((sVE, p1_done_ve))
                        first_extra.extend(extra_first_waits)
                        extra_first_waits = ()
                    nmm = len(sls) * 2
                    cnt = 0
                    for j in range(2):
                        for si, sl in enumerate(sls):
                            cnt += 1
                            inc = sPE if cnt == nmm else None
                            r = mm_of_fc(fc, j, sl, banksel(j, si),
                                         start=(fc == 0), stop=(fc == NFC - 1),
                                         extra_w=first_extra if cnt == 1 else (), inc=inc)
                    fcuse[(gidx, fc)] = r
                last_pe = fcuse[(gidx, NFC - 1)]
                for j in range(2):
                    for si, sl in enumerate(sls):
                        of = of0 + j
                        slotid = (j * len(sls) + si) if kind in ("v", "wo") else (j * 4 + si)
                        w = [(sPE, last_pe)]
                        pv = stg_store.get(slotid)
                        if pv:
                            w.append((sS, pv))
                        bias = bqkv[:, of:of + 1] if kind != "wo" else bob[:, of:of + 1]
                        ae = P.op("scalar", (lambda bank=banksel(j, si), slotid=slotid, bias=bias:
                                  (lambda e: e.activation(stg[:, slotid * 512:(slotid + 1) * 512],
                                                          bank, AF.Identity, bias=bias, scale=1.0)))(),
                                  waits=w, inc=sAE)
                        if kind in ("k", "q"):
                            base = q_scr if kind == "q" else kcc_in
                            r0 = (of - (32 if kind == "k" else 0)) * 128
                            sc = P.op("scalar", (lambda base=base, r0=r0, sl=sl, slotid=slotid:
                                      (lambda e: e.dma_start(
                                          out=base[r0:r0 + 128, sl * 512:(sl + 1) * 512],
                                          in_=stg[:, slotid * 512:(slotid + 1) * 512])))(),
                                      waits=[(sAE, ae)], inc=sS, inc_amt=16)
                            stg_store[slotid] = sc
                        else:
                            tb = pE if (j + si) % 2 == 0 else pF
                            tslot = 4 + (j * len(sls) + si) % 4
                            tpe = None
                            for i4 in range(4):
                                wv = [(sAE, ae)] if i4 == 0 else []
                                pvv = tr_state["last_evac"].get(("p2t", (j + si) % 2))
                                if i4 == 0 and pvv:
                                    wv.append((sVE, pvv))
                                tpe = P.op("tensor", (lambda tb=tb, i4=i4, slotid=slotid:
                                           (lambda e: e.matmul(tb[:, i4 * 128:(i4 + 1) * 128],
                                                               stg[:, slotid * 512 + i4 * 128: slotid * 512 + (i4 + 1) * 128],
                                                               eye[:], start=True, stop=True)))(),
                                           waits=wv, inc=(sPE if i4 == 3 else None))
                            wv2 = [(sPE, tpe)]
                            pv2 = stg_store.get(tslot)
                            if pv2:
                                wv2.append((sS, pv2))
                            ve = P.op("vector", (lambda tb=tb, tslot=tslot: (lambda e: e.tensor_copy(
                                stg[:, tslot * 512:(tslot + 1) * 512], tb[:])))(),
                                waits=wv2, inc=sVE)
                            tr_state["last_evac"][("p2t", (j + si) % 2)] = ve
                            for i4 in range(4):
                                tt = sl * 4 + i4
                                if kind == "v":
                                    blk = vcc_in[(of - 64) * 16 + tt]
                                else:
                                    blk = attn_scr[tt * 32 + of]
                                sc = P.op("scalar", (lambda blk=blk, tslot=tslot, i4=i4:
                                          (lambda e: e.dma_start(
                                              out=blk,
                                              in_=stg[:, tslot * 512 + i4 * 128: tslot * 512 + (i4 + 1) * 128])))(),
                                          waits=[(sVE, ve)], inc=sS, inc_amt=16)
                            stg_store[tslot] = sc
                evac_ae[(gidx, pi)] = sAE.n
            evac_ae[(gidx, "last")] = sAE.n
            return gidx

        for gof in range(16):
            qkv_group(64 + gof * 2, "v")
        v_stores_done = sS.n
        cc_v = None
        for gof in range(16):
            qkv_group(32 + gof * 2, "k")
        k_stores_done = sS.n
        cc_v = None
        cc_k = None
        for gof in range(16):
            qkv_group(gof * 2, "q", last_in_segment=(gof == 15))
        q_stores_done = sS.n
        cc_v = P.op("gpsimd", lambda e: e.collective_compute(
            "AllGather", ALU.bypass,
            replica_groups=[[0, 1], [2, 3], [4, 5], [6, 7]],
            ins=[vcc_in_t.ap().opt()], outs=[vcc_out_t.ap().opt()]),
            waits=[(sS, q_stores_done)], inc=sCC)
        cc_k = P.op("gpsimd", lambda e: e.collective_compute(
            "AllGather", ALU.bypass,
            replica_groups=[[0, 1], [2, 3], [4, 5], [6, 7]],
            ins=[kcc_in_t.ap().opt()], outs=[kcc_out_t.ap().opt()]),
            waits=[(sS, q_stores_done)], inc=sCC)
        p2_last_pe = sPE.n
        p2_last_ae = sAE.n

        if DEBUG:
            P.op("sync", lambda e: e.dma_start(out=dbg["d_q"], in_=q_scr),
                 waits=[(sS, q_stores_done)], inc=sL, inc_amt=16)
            P.op("sync", lambda e: e.dma_start(out=dbg["d_k"], in_=kcc_in),
                 waits=[(sS, k_stores_done)], inc=sL, inc_amt=16)
            P.op("sync", lambda e: e.dma_start(out=dbg["d_v"], in_=vcc_in),
                 waits=[(sS, v_stores_done)], inc=sL, inc_amt=16)
            P.op("sync", lambda e: e.dma_start(out=dbg["d_kg"], in_=kcc_out),
                 waits=[(sCC, cc_k)], inc=sL, inc_amt=16)

        kv_ld = {}
        s_done = {}
        exp_done = {}
        ctx_done = {}
        smm_last = {}
        ctx_evac = {}
        head_last_mm = {}
        bc_copy = {}
        qld = {}

        for h in range(HEADS):
            w = [(sCC, cc_k), (sS, q_stores_done)]
            if h == 0:
                w.append((sPE, p2_last_pe))
            else:
                w.append((sPE, head_last_mm[h - 1]))
            for r in range(2):
                P.op("sync", (lambda h=h, r=r: (lambda e: e.dma_start(
                    out=dbuf[:, r * 2048:(r + 1) * 2048],
                    in_=kcc_out[r * DM + h * 128: r * DM + (h + 1) * 128, :])))(),
                    waits=w if r == 0 else [], inc=sL, inc_amt=16)
            for r in range(2):
                P.op("sync", (lambda h=h, r=r: (lambda e: e.dma_start(
                    out=dbuf[:, 4096 + r * 2048: 4096 + (r + 1) * 2048].rearrange(
                        "p (t j) -> p t j", j=128),
                    in_=vcc_out[r * 512 + h * 16: r * 512 + (h + 1) * 16].rearrange(
                        "t p j -> p t j"))))(),
                    waits=[(sCC, cc_v)] if r == 0 else [], inc=sL, inc_amt=16)
            kv_ld[h] = echo("sync", sL)

            for qt in range(4):
                i = h * 4 + qt
                wq = []
                if i >= 2 and smm_last.get(i - 2):
                    wq.append((sPE, smm_last[i - 2]))
                P.op("sync", (lambda h=h, qt=qt, i=i: (lambda e: e.dma_start(
                    out=qbuf[:, (i % 2) * 512:(i % 2) * 512 + 512],
                    in_=q_scr[h * 128:(h + 1) * 128, qt * 512:(qt + 1) * 512])))(),
                    waits=wq, inc=sL, inc_amt=16)
                qld[i] = echo("sync", sL)

                ctxbank = pC if i % 2 == 0 else pD
                lbank = pE if i % 2 == 0 else pF

                def emit_s(jp, i=i, h=h):
                    par = jp % 2
                    sb_ = pA if par == 0 else pB
                    gpair = i * 16 + jp
                    w1 = []
                    pe_prev = exp_done.get(gpair - 2)
                    if pe_prev:
                        w1.append((sAE, pe_prev))
                    if jp == 0:
                        w1 += [(sL, qld[i]), (sL, kv_ld[h])]
                        pv = ctx_evac.get(i - 2)
                        if pv:
                            w1.append((sVE, pv))
                        pv = bc_copy.get(i - 1)
                        if pv:
                            w1.append((sVE, pv))
                        if i == 0:
                            w1.append((sPO, setup_done))
                    r = None
                    for half in range(2):
                        kc = jp * 2 + half
                        rsel, idx = kc // 16, kc % 16
                        r = P.op("tensor", (lambda sb_=sb_, half=half, rsel=rsel, idx=idx, i=i:
                                 (lambda e: e.matmul(
                                     sb_[:, half * 512:(half + 1) * 512],
                                     dbuf[:, rsel * 2048 + idx * 128: rsel * 2048 + (idx + 1) * 128],
                                     qbuf[:, (i % 2) * 512:(i % 2) * 512 + 512],
                                     start=True, stop=True)))(),
                                 waits=w1 if half == 0 else (), inc=(sPE if half == 1 else None))
                    s_done[gpair] = r
                    smm_last[i] = r

                def emit_exp(jp, i=i):
                    par = jp % 2
                    sb_ = pA if par == 0 else pB
                    gpair = i * 16 + jp
                    w2 = [(sPE, s_done[gpair])]
                    pv = ctx_done.get(gpair - 2)
                    if pv:
                        w2.append((sPE, pv))
                    exp_done[gpair] = P.op("scalar", (lambda sb_=sb_, par=par: (lambda e: e.activation(
                        pbuf[:, par * 1024:(par + 1) * 1024], sb_[:],
                        AF.Exp, bias=ESHIFT, scale=SM_SCALE)))(),
                        waits=w2, inc=sAE)

                def emit_ctx(jp, i=i, ctxbank=ctxbank, lbank=lbank):
                    par = jp % 2
                    gpair = i * 16 + jp
                    w3 = [(sAE, exp_done[gpair])]
                    lastmm = None
                    for half in range(2):
                        kc = jp * 2 + half
                        rsel, idx = kc // 16, kc % 16
                        P.op("tensor", (lambda ctxbank=ctxbank, par=par, half=half, rsel=rsel, idx=idx, kc=kc:
                             (lambda e: e.matmul(
                                 ctxbank[:], dbuf[:, 4096 + rsel * 2048 + idx * 128: 4096 + rsel * 2048 + (idx + 1) * 128],
                                 pbuf[:, par * 1024 + half * 512: par * 1024 + (half + 1) * 512],
                                 start=(kc == 0), stop=(kc == 31))))(),
                             waits=w3 if half == 0 else ())
                    for half in range(2):
                        kc = jp * 2 + half
                        lastmm = P.op("tensor", (lambda lbank=lbank, par=par, half=half, kc=kc:
                                      (lambda e: e.matmul(
                                          lbank[0:1, :], ones[:],
                                          pbuf[:, par * 1024 + half * 512: par * 1024 + (half + 1) * 512],
                                          start=(kc == 0), stop=(kc == 31))))(),
                                      inc=(sPE if half == 1 else None))
                    ctx_done[gpair] = lastmm

                for jp in range(16):
                    emit_s(jp)
                    emit_exp(jp)
                    if jp >= 1:
                        emit_ctx(jp - 1)
                emit_ctx(15)
                vr = P.op("vector", (lambda lbank=lbank: (lambda e: e.reciprocal(
                    rrowf[:], lbank[0:1, :])))(), waits=[(sPE, ctx_done[i * 16 + 15])], inc=sVE)
                ve1 = P.op("scalar", lambda e: e.activation(rrow[:], rrowf[:],
                           AF.Identity, bias=0.0, scale=1.0),
                           waits=[(sVE, vr)], inc=sVE)
                wb = [(sVE, ve1), (sAE, exp_done[i * 16 + 14])]
                bpe = P.op("tensor", lambda e: e.matmul(
                    pA[:, 0:512], onesr[:], rrow[:], start=True, stop=True),
                    waits=wb, inc=sPE)
                ve2 = P.op("vector", lambda e: e.tensor_copy(bcast, pA[:, 0:512]),
                           waits=[(sPE, bpe)], inc=sVE)
                bc_copy[i] = ve2
                cx = P.op("vector", (lambda ctxbank=ctxbank, h=h, qt=qt: (lambda e: e.tensor_mul(
                    big[:, h * TOK + qt * 512: h * TOK + (qt + 1) * 512],
                    ctxbank[:], bcast)))(), waits=[(sVE, ve2)], inc=sVE)
                ctx_evac[i] = P.op("vector", (lambda h=h, qt=qt: (lambda e: e.tensor_copy(
                    cnf[:], big[:, h * TOK + qt * 512 + 496: h * TOK + qt * 512 + 512])))(),
                    waits=[(sVE, cx)], inc=sVE)
                head_last_mm[h] = ctx_done[i * 16 + 15]
        p3_done_ve = sVE.n
        p3_done_pe = sPE.n

        if DEBUG:
            P.op("scalar", lambda e: e.dma_start(out=dbg["d_ctx"], in_=big[:]),
                 waits=[(sVE, p3_done_ve)], inc=sS, inc_amt=16)

        p3_seal = P.op("scalar", lambda e: e.activation(
            big[:, 31 * TOK + 3 * 512: 31 * TOK + 4 * 512],
            big[:, 31 * TOK + 3 * 512: 31 * TOK + 4 * 512],
            AF.Identity, bias=0.0, scale=1.0), waits=[(sVE, sVE.n)], inc=sAE)
        p3_exp_last = sAE.n
        for gof in range(16):
            qkv_group(gof * 2, "wo",
                      extra_first_waits=((sVE, p3_done_ve), (sAE, p3_exp_last)) if gof == 0 else (),
                      cast_extra=((sPE, p3_done_pe),) if gof == 0 else (),
                      last_in_segment=(gof == 15),
                      load_extra=((sPE, p3_done_pe), (sVE, p3_done_ve)) if gof == 0 else ())
        p4_stores_done = sS.n
        p4_last_pe = sPE.n

        if DEBUG:
            P.op("sync", lambda e: e.dma_start(out=dbg["d_attn"], in_=attn_scr),
                 waits=[(sS, p4_stores_done)], inc=sL, inc_amt=16)

        x2_store = {}
        ln2_ae = {}
        ln3_ae = {}
        add_ve = {}
        for tt in range(NT):
            wa = [(sS, p4_stores_done)] if tt == 0 else [(sAE, ln2_ae[tt - 1])]
            al = P.op("sync", (lambda tt=tt: (lambda e: e.dma_start(
                out=stg[:].rearrange("p (t j) -> p t j", j=128),
                in_=attn_scr[tt * 32:(tt + 1) * 32].rearrange("t p j -> p t j"))))(),
                waits=wa, inc=sL, inc_amt=16)
            al = echo("sync", sL)
            wx = []
            if tt >= 1:
                wx.append((sAE, ln3_ae[tt - 1]))
                wx.append((sS, x2_store[tt - 1]))
            xl = P.op("sync", (lambda tt=tt: (lambda e: e.dma_start(
                out=bufC[:], in_=x_in[tt * 128:(tt + 1) * 128, :])))(),
                waits=wx, inc=sL, inc_amt=16)
            xl = echo("sync", sL)
            wd = [(sVE, add_ve[tt - 1])] if tt >= 1 else []
            ln2_ae[tt] = emit_ln_stats_apply(stg, bufD, ready_waits=[(sL, al)],
                                             apply_extra_waits=wd)
            add_ve[tt] = P.op("vector", lambda e: e.tensor_add(bufC[:], bufC[:], bufD[:]),
                              waits=[(sAE, ln2_ae[tt]), (sL, xl)], inc=sVE)
            x2_store[tt] = P.op("scalar", (lambda tt=tt: (lambda e: e.dma_start(
                out=x2_scr[tt * 128:(tt + 1) * 128, :], in_=bufC[:])))(),
                waits=[(sVE, add_ve[tt])], inc=sS, inc_amt=16)
            aw = [(sPE, tr_state["mm"])] if tr_state["mm"] else []
            ln3_ae[tt] = emit_ln_stats_apply(bufC, wnk, ready_waits=[(sVE, add_ve[tt])],
                                             apply_extra_waits=aw)
            first = [(sAE, ln3_ae[tt])]
            if tt == 0:
                first.append((sPE, p4_last_pe))
            mmn, _ = emit_transpose_to_big(wnk, tt, first_waits=first)
            tr_state["mm"] = mmn
        p5_done_ve = sVE.n
        p5_last_ae = sAE.n
        x2_all = sS.n

        if DEBUG:
            P.op("scalar", lambda e: e.dma_start(out=dbg["d_x2"], in_=x2_scr),
                 waits=[(sS, x2_all)], inc=sS, inc_amt=16)

        qw_ld = {}
        deq_done = {}
        wT_evac = {}
        ffn_mm_done = {}
        ffn_stores = {}
        for n in range(NFC):
            w = []
            if n == 0:
                w = [(sS, x2_all), (sAE, p5_last_ae), (sVE, p5_done_ve)]
            else:
                w = [(sVE, deq_done[n - 1])]
            P.op("gpsimd", (lambda n=n: (lambda e: e.dma_start(
                out=bufC[:, 0:3072], in_=qwq[n * 128:(n + 1) * 128, :])))(),
                waits=w, inc=sG, inc_amt=16)
            qw_ld[n] = echo("gpsimd", sG)
            wv = [(sG, qw_ld[n])]
            if tr_state["mm"]:
                wv.append((sPE, tr_state["mm"]))
            P.op("vector", lambda e: e.tensor_scalar(
                bufD[:, 0:3072], bufC[:, 0:3072], 1.0 / 64.0, -0.4921875,
                ALU.mult, ALU.add), waits=wv)
            P.op("vector", lambda e: e.tensor_scalar(
                bufD[:, 0:3072], bufD[:, 0:3072], C_ROUND, -C_ROUND,
                ALU.add, ALU.add))
            P.op("vector", lambda e: e.scalar_tensor_tensor(
                bufC[:, 0:3072], bufD[:, 0:3072], -64.0, bufC[:, 0:3072],
                ALU.mult, ALU.add))

            vm = P.op("vector", (lambda n=n: (lambda e: e.tensor_scalar(
                mv[:, 6:7], bzp[:, n:n + 1], bsc[:, n:n + 1], -1.0,
                ALU.mult, ALU.mult)))(), inc=sVE)

            def wv_(src, j, outj, n=n):
                sv = src[:, 0:3072].rearrange("p (g w) -> p g w", w=3)[:, :, j]
                ov = wnk[:, 0:DM].rearrange("p (g f) -> p g f", f=4)[:, :, outj]
                def f(e):
                    return e.activation(ov, sv, AF.Identity,
                                        bias=mv[:, 6:7], scale=bsc[:, n:n + 1])
                return f
            P.op("scalar", wv_(bufC, 0, 0), waits=[(sVE, vm)])
            P.op("scalar", wv_(bufD, 0, 1))
            P.op("scalar", wv_(bufC, 1, 2))
            deq_done[n] = P.op("scalar", wv_(bufC, 2, 3), inc=sVE)

            tslot = n % 2
            last_pe = None
            for g in range(8):
                bank = pC if g % 2 == 0 else pD
                for j in range(4):
                    fc = g * 4 + j
                    w = []
                    if fc == 0:
                        w.append((sVE, deq_done[n]))
                        if n >= 2 and ffn_mm_done.get(n - 2):
                            w.append((sPE, ffn_mm_done[n - 2]))
                        if n == 0:
                            w.append((sPE, p4_last_pe))
                    if j == 0:
                        pv = tr_state["last_evac"].get(("p6", g % 2))
                        if pv:
                            w.append((sVE, pv))
                    last_pe = P.op("tensor", (lambda bank=bank, j=j, fc=fc: (lambda e: e.matmul(
                        bank[:, j * 128:(j + 1) * 128], wnk[:, fc * 128:(fc + 1) * 128],
                        eye[:], start=True, stop=True)))(),
                        waits=w, inc=(sPE if j == 3 else None))
                ve = P.op("scalar", (lambda bank=bank, g=g, tslot=tslot: (lambda e: e.activation(
                    dbuf[:, tslot * 4096 + g * 512:tslot * 4096 + (g + 1) * 512], bank[:],
                    AF.Identity, bias=0.0, scale=1.0)))(),
                    waits=[(sPE, last_pe)], inc=sVE)
                tr_state["last_evac"][("p6", g % 2)] = ve
            wT_evac[n] = ve
            tr_state["mm"] = last_pe

            last = None
            for kc in range(NFC):
                for sl in range(4):
                    w = []
                    if kc == 0 and sl == 0:
                        w.append((sVE, wT_evac[n]))
                        pv = ffn_stores.get(n - 1)
                        if pv:
                            w.append((sAE, ffn_stores[n - 1][0]))
                    last = P.op("tensor", (lambda kc=kc, sl=sl, tslot=tslot: (lambda e: e.matmul(
                        BANKS4[sl], dbuf[:, tslot * 4096 + kc * 128: tslot * 4096 + (kc + 1) * 128],
                        fm(big, kc, sl * 512, (sl + 1) * 512),
                        start=(kc == 0), stop=(kc == NFC - 1))))(),
                        waits=w, inc=(sPE if (kc == NFC - 1 and sl == 3) else None))
            ffn_mm_done[n] = last

            ae_last = None
            for sl in range(4):
                w = [(sPE, ffn_mm_done[n])]
                pv = stg_store.get(("p6", sl))
                if pv:
                    w.append((sS, pv))
                ae = P.op("scalar", (lambda sl=sl, n=n: (lambda e: e.activation(
                    stg[:, sl * 512:(sl + 1) * 512], BANKS4[sl], AF.Identity,
                    bias=bfb[:, n:n + 1], scale=1.0)))(), waits=w, inc=sAE)
                tb = pE if sl % 2 == 0 else pF
                tpe = None
                for i4 in range(4):
                    w2 = [(sAE, ae)] if i4 == 0 else []
                    pv2 = tr_state["last_evac"].get(("p6o", sl % 2))
                    if i4 == 0 and pv2:
                        w2.append((sVE, pv2))
                    tpe = P.op("tensor", (lambda tb=tb, i4=i4, sl=sl: (lambda e: e.matmul(
                        tb[:, i4 * 128:(i4 + 1) * 128],
                        stg[:, sl * 512 + i4 * 128: sl * 512 + (i4 + 1) * 128],
                        eye[:], start=True, stop=True)))(),
                        waits=w2, inc=(sPE if i4 == 3 else None))
                w3 = [(sPE, tpe)]
                pv3 = stg_store.get(("p6b", sl))
                if pv3:
                    w3.append((sS, pv3))
                ve = P.op("vector", (lambda tb=tb, sl=sl: (lambda e: e.tensor_copy(
                    stg[:, (4 + sl) * 512:(5 + sl) * 512], tb[:])))(),
                    waits=w3, inc=sVE)
                tr_state["last_evac"][("p6o", sl % 2)] = ve
                sc = None
                for i4 in range(4):
                    tt = sl * 4 + i4
                    sc = P.op("scalar", (lambda tt=tt, n=n, sl=sl, i4=i4: (lambda e: e.dma_start(
                        out=ffn_scr[tt * 32 + n],
                        in_=stg[:, (4 + sl) * 512 + i4 * 128: (4 + sl) * 512 + (i4 + 1) * 128])))(),
                        waits=[(sVE, ve)], inc=sS, inc_amt=16)
                stg_store[("p6", sl)] = sS.n
                stg_store[("p6b", sl)] = sc
                ae_last = ae
            ffn_stores[n] = (ae_last, sS.n)
        p6_stores_done = sS.n

        if DEBUG:
            P.op("sync", lambda e: e.dma_start(out=dbg["d_ffn"], in_=ffn_scr),
                 waits=[(sS, p6_stores_done)], inc=sL, inc_amt=16)

        ln4_ae = {}
        out_store = {}
        add7 = {}
        for tt in range(NT):
            wa = [(sS, p6_stores_done)] if tt == 0 else [(sAE, ln4_ae[tt - 1])]
            fl = P.op("sync", (lambda tt=tt: (lambda e: e.dma_start(
                out=stg[:].rearrange("p (t j) -> p t j", j=128),
                in_=ffn_scr[tt * 32:(tt + 1) * 32].rearrange("t p j -> p t j"))))(),
                waits=wa, inc=sL, inc_amt=16)
            fl = echo("sync", sL)
            wx = []
            if tt >= 1:
                wx.append((sVE, add7[tt - 1]))
            xl = P.op("sync", (lambda tt=tt: (lambda e: e.dma_start(
                out=bufC[:], in_=x2_scr[tt * 128:(tt + 1) * 128, :])))(),
                waits=wx, inc=sL, inc_amt=16)
            xl = echo("sync", sL)
            wd = [(sS, out_store[tt - 1])] if tt >= 1 else []
            ln4_ae[tt] = emit_ln_stats_apply(stg, bufD, ready_waits=[(sL, fl)],
                                             apply_extra_waits=wd)
            add7[tt] = P.op("vector", lambda e: e.tensor_add(bufD[:], bufC[:], bufD[:]),
                            waits=[(sAE, ln4_ae[tt]), (sL, xl)], inc=sVE)
            out_store[tt] = P.op("scalar", (lambda tt=tt: (lambda e: e.dma_start(
                out=out[tt * 128:(tt + 1) * 128, :], in_=bufD[:])))(),
                waits=[(sVE, add7[tt])], inc=sS, inc_amt=16)

        final_sS = sS.n
        P.raw("sync", lambda e: e.wait_ge(sS.h, final_sS))
        P.raw("gpsimd", lambda e: e.wait_ge(sS.h, final_sS))

        with nc.Block() as block:
            @block.sync
            def _(e):
                for f in P.p["sync"]:
                    f(e)

            @block.gpsimd
            def _(e):
                for f in P.p["gpsimd"]:
                    f(e)

            @block.tensor
            def _(e):
                for f in P.p["tensor"]:
                    f(e)

            @block.vector
            def _(e):
                for f in P.p["vector"]:
                    f(e)

            @block.scalar
            def _(e):
                for f in P.p["scalar"]:
                    f(e)
    return nc


_NC_CACHE = [None]


def kernel(L0_attn_x, L0_attn_attn_qkvw, L0_attn_attn_qkvb, L0_attn_attn_ow,
           L0_attn_attn_ob, L0_ffn_qweight, L0_ffn_weight_scale,
           L0_ffn_weight_zero_point, L0_ffn_bias):
    kernel._LAST = None
    x = np.ascontiguousarray(np.asarray(L0_attn_x, dtype=np.float32))
    wqkv = np.ascontiguousarray(np.asarray(L0_attn_attn_qkvw, dtype=np.float32))
    qkvb = np.asarray(L0_attn_attn_qkvb, dtype=np.float32)
    wo = np.ascontiguousarray(np.asarray(L0_attn_attn_ow, dtype=np.float32))
    ob = np.asarray(L0_attn_attn_ob, dtype=np.float32)
    qw = np.ascontiguousarray(np.asarray(L0_ffn_qweight, dtype=np.int32))
    wsc = np.asarray(L0_ffn_weight_scale, dtype=np.float32).reshape(-1)
    wzp = np.asarray(L0_ffn_weight_zero_point, dtype=np.float32).reshape(-1)
    fb = np.asarray(L0_ffn_bias, dtype=np.float32)

    qkvb_t = np.ascontiguousarray(qkvb.reshape(NOF, 128).T)
    ob_t = np.ascontiguousarray(ob.reshape(NFC, 128).T)
    fb_t = np.ascontiguousarray(fb.reshape(NFC, 128).T)
    wsc_t = np.ascontiguousarray(wsc.reshape(NFC, 128).T)
    wzp_t = np.ascontiguousarray(wzp.reshape(NFC, 128).T)

    if _NC_CACHE[0] is None:
        _NC_CACHE[0] = build()
    nc = _NC_CACHE[0]

    in_maps = []
    for c in range(8):
        b, half = c // 2, c % 2
        xs = np.ascontiguousarray(x[b, half * TOK:(half + 1) * TOK, :])
        in_maps.append({
            "x": xs, "wqkv": wqkv, "wo": wo, "qw": qw,
            "qkvb_t": qkvb_t, "ob_t": ob_t, "fb_t": fb_t,
            "wsc_t": wsc_t, "wzp_t": wzp_t,
        })
    res = run_bass_kernel_spmd(nc, in_maps, core_ids=list(range(8)))
    kernel._LAST = res
    outp = np.empty((4, 4096, DM), dtype=np.float32)
    for c in range(8):
        b, half = c // 2, c % 2
        outp[b, half * TOK:(half + 1) * TOK, :] = res.results[c]["out"]
    return outp
